# version 06
# baseline (speedup 1.0000x reference)
import math
import numpy as np
import ml_dtypes
from contextlib import ExitStack

import concourse.bass as bass
import concourse.mybir as mybir
from concourse.bass_utils import run_bass_kernel_spmd

F32 = mybir.dt.float32
BF16 = mybir.dt.bfloat16
AF = mybir.ActivationFunctionType
ALU = mybir.AluOpType
AX = mybir.AxisListType.X
BF = ml_dtypes.bfloat16

S = 4096
DM = 4096
H = 32
NOPE = 64
ROPE = 64
VD = 128
QHD = 128
KVR = 256
LATW = 320
NSLOT = 8
TRIPS = [1, 2, 3, 4, 1, 2, 3, 4]
EPS_RMS = 1e-6
EPS_LN = 1e-5
M_YARN = 0.1 * 1.0 * math.log(128.0) + 1.0
SOFTMAX_SCALE = (QHD ** -0.5) * M_YARN * M_YARN
FFN_SCALING = 2.0
VSTRIDE = 132

ENGS = ("sp", "act", "dve", "pool", "pe")
KB = 1024


def _yarn_cos_sin(seq_len, dim=ROPE, base=10000.0, factor=128.0, beta_fast=32.0,
                  beta_slow=1.0, orig_max=8192, mscale=1.0, mscale_all_dim=1.0):
    def corr_dim(r):
        return dim * math.log(orig_max / (r * 2 * math.pi)) / (2 * math.log(base))
    low = max(math.floor(corr_dim(beta_fast)), 0)
    high = min(math.ceil(corr_dim(beta_slow)), dim - 1)
    hi = high + 0.001 if low == high else float(high)
    ramp = np.clip((np.arange(dim // 2, dtype=np.float32) - low) / (hi - low), 0.0, 1.0)
    inv_freq_mask = 1.0 - ramp
    freq_extra = 1.0 / base ** (np.arange(0, dim, 2, dtype=np.float32) / dim)
    freq_inter = freq_extra / factor
    inv_freq = freq_inter * (1.0 - inv_freq_mask) + freq_extra * inv_freq_mask
    t = np.arange(seq_len, dtype=np.float32)
    freqs = np.outer(t, inv_freq)
    emb = np.concatenate([freqs, freqs], axis=-1)

    def gm(s, m):
        return 1.0 if s <= 1 else 0.1 * m * math.log(s) + 1.0
    ms = gm(factor, mscale) / gm(factor, mscale_all_dim)
    return (np.cos(emb) * ms).astype(np.float32), (np.sin(emb) * ms).astype(np.float32)


def _slot_tiles(j):
    return [8 * i + j for i in range(4)] + [8 * i + (7 - j) for i in range(4)]


def _deint_perm(n=64):
    p = np.empty(n, dtype=np.int64)
    p[:32] = np.arange(32) * 2
    p[32:] = np.arange(32) * 2 + 1
    return p


def _rot_mat():
    R = np.zeros((64, 64), dtype=np.float32)
    for i in range(32):
        R[32 + i, i] = -1.0
        R[i, 32 + i] = 1.0
    return R.astype(BF)


def prepare_inputs(inp):
    x = np.asarray(inp["L0_attn_x"])
    wq_a = np.asarray(inp["L0_attn_wq_a"])
    q_lnw = np.asarray(inp["L0_attn_q_a_ln_w"])
    wq_b = np.asarray(inp["L0_attn_wq_b"])
    wkv_a = np.asarray(inp["L0_attn_wkv_a"])
    kv_lnw = np.asarray(inp["L0_attn_kv_a_ln_w"])
    wkv_b = np.asarray(inp["L0_attn_wkv_b"])
    wo = np.asarray(inp["L0_attn_wo"])
    A_w = np.asarray(inp["L0_ffn_A_w"])
    B_w = np.asarray(inp["L0_ffn_B_w"])
    Wq_w = np.asarray(inp["L0_ffn_Wq_w"])
    keys = np.asarray(inp["L0_ffn_keys"])

    C = np.ascontiguousarray
    perm = _deint_perm()

    wkv_a_p = wkv_a.copy()
    wkv_a_p[KVR:LATW] = wkv_a_p[KVR:LATW][perm]
    wkvaT = C(wkv_a_p.T)
    wqaT = C(wq_a.T)
    wq_b_p = wq_b.reshape(H, QHD, 1024).copy()
    wq_b_p[:, NOPE:] = wq_b_p[:, NOPE + perm]
    wqbT = C(wq_b_p.reshape(H * QHD, 1024).T)
    wkvbT = C(wkv_b.T)
    wo_sta = C(np.transpose(wo.reshape(32, 128, 32, 128), (0, 3, 2, 1)))
    A_sta = C(A_w.T.reshape(32, 128, 512))
    B_sta = C(B_w.T.reshape(4, 128, 4096))
    WqT = C(Wq_w.T.reshape(32, 128, 16))
    keysT = C(keys.T)
    lnw_q = C(q_lnw.reshape(8, 128).T)
    lnw_kv = C(kv_lnw.reshape(2, 128).T)

    cos, sin = _yarn_cos_sin(S)
    cosF = C(cos.T)
    sinF = C(sin.T)

    ident16 = np.eye(128, dtype=BF)
    ident32 = np.eye(128, dtype=np.float32)
    rot = _rot_mat()
    ones_col32 = np.ones((128, 1), dtype=np.float32)
    ones_col16 = np.ones((128, 1), dtype=BF)
    ones_row16 = np.ones((1, 512), dtype=BF)
    ones_row32 = np.ones((1, 128), dtype=np.float32)
    cvec = np.zeros((128, 8), dtype=np.float32)
    cvec[:, 0] = SOFTMAX_SCALE
    cvec[:, 1] = -0.5
    R_exp = np.zeros((64, 512), dtype=np.float32)
    for c in range(4):
        for m in range(128):
            R_exp[16 * c + m // 8, 128 * c + m] = 1.0

    shared = dict(
        wkvaT=wkvaT, wqaT=wqaT, wqbT=wqbT, wkvbT=wkvbT, wo_sta=wo_sta,
        A_sta=A_sta, B_sta=B_sta, WqT=WqT, keysT=keysT,
        lnw_q=lnw_q, lnw_kv=lnw_kv, cosF=cosF, sinF=sinF,
        ident16=ident16, ident32=ident32, rot=rot,
        ones_col32=ones_col32, ones_col16=ones_col16,
        ones_row16=ones_row16, ones_row32=ones_row32, cvec=cvec, R_exp=R_exp,
    )

    in_maps = []
    for core in range(8):
        b, jj = core // 4, core % 4
        tiles = _slot_tiles(jj)
        cols = np.concatenate([np.arange(t * 128, (t + 1) * 128) for t in tiles])
        xT = C(x[b].T)
        xoT = C(xT[:, cols])
        cos_o = C(cos[cols].T)
        sin_o = C(sin[cols].T)
        maskPT = np.zeros((2, 128, 1024), dtype=BF)
        for fam, toff in enumerate((jj, 7 - jj)):
            for sblk in range(8):
                for k in range(128):
                    lo = (sblk * 128 + k) - toff * 128
                    if lo <= 0:
                        maskPT[fam, k, sblk * 128:(sblk + 1) * 128] = BF(1.0)
                    elif lo < 128:
                        maskPT[fam, k, sblk * 128 + lo:(sblk + 1) * 128] = BF(1.0)
        m = dict(shared)
        m.update(xT=xT, xoT=xoT, cos_o=cos_o, sin_o=sin_o,
                 maskPT=C(maskPT.reshape(2 * 128, 1024)))
        in_maps.append(m)
    return in_maps


_LAST_EM = [None]


class Em:
    def __init__(self, nc, ctx):
        self.nc = nc
        self.ctx = ctx
        self.stream = {e: [] for e in ENGS}
        self.ms = {e: 0 for e in ENGS}
        self.ndma = {}
        self.sem_ms = {e: ctx.enter_context(nc.semaphore(f"ms_{e}")) for e in ENGS}
        self.sem_dma = {}
        self.waited = {}

    TRACE_LINES = False

    def op(self, e, meth, *args, **kw):
        self.stream[e].append((meth, args, kw, None, 0, (), self._ln()))

    def _ln(self):
        if not Em.TRACE_LINES:
            return 0
        import sys as _s
        f = _s._getframe(2)
        return f.f_lineno

    def opw(self, e, toks, meth, *args, **kw):
        att = []
        for t in toks:
            if t is None:
                continue
            kind, p, v = t
            sem = self.sem_ms[p] if kind == "ms" else self.sem_dma[p]
            val = v if kind == "ms" else 16 * v
            att.append((sem, val))
        self.stream[e].append((meth, args, kw, None, 0, att, self._ln()))

    def mark(self, e):
        self.ms[e] += 1
        self.stream[e].append(("drain", (), {}, self.sem_ms[e], 1, (), self._ln()))
        return ("ms", e, self.ms[e])

    def self_edge(self, e):
        t = self.mark(e)
        self.op(e, "wait_ge", self.sem_ms[e], t[2])

    def dma(self, e, grp, out_ap, in_ap, **kw):
        if grp not in self.sem_dma:
            self.sem_dma[grp] = self.ctx.enter_context(self.nc.semaphore(f"dq_{grp}"))
            self.ndma[grp] = 0
        self.ndma[grp] += 1
        self.stream[e].append(("dma_start", (out_ap, in_ap), kw, self.sem_dma[grp], 16, (), self._ln()))
        return ("dma", grp, self.ndma[grp])

    def wait(self, e, tok):
        if tok is None:
            return
        kind, p, v = tok
        if kind == "ms" and p == e:
            return
        key = (e, kind, p)
        if self.waited.get(key, 0) >= v:
            return
        self.waited[key] = v
        sem = self.sem_ms[p] if kind == "ms" else self.sem_dma[p]
        val = v if kind == "ms" else 16 * v
        self.op(e, "wait_ge", sem, val)

    def waits(self, e, toks):
        best = {}
        for t in toks:
            if t is None:
                continue
            k = (t[0], t[1])
            if k not in best or best[k][2] < t[2]:
                best[k] = t
        for t in best.values():
            self.wait(e, t)

    def barrier(self):
        toks = [self.mark(e) for e in ENGS]
        for e in ENGS:
            self.waits(e, toks)
        return dict(zip(ENGS, toks))

    def preamble(self, tok):
        for e in ENGS:
            self.wait(e, tok)
            for _ in range(10):
                self.op(e, "nop")


def _replay(nc, em, ctx):
    block = ctx.enter_context(nc.Block())
    sections = {"sp": block.sync, "act": block.scalar, "dve": block.vector,
                "pool": block.gpsimd, "pe": block.tensor}

    def runner(e):
        def body(h):
            for entry in em.stream[e]:
                meth, args, kw, sem, n = entry[:5]
                ins = getattr(h, meth)(*args, **kw)
                for s, v in (entry[5] if len(entry) > 5 else ()):
                    ins._wait_ge(s, v)
                if sem is not None:
                    ins.then_inc(sem, n)
        return body
    for e in ENGS:
        sections[e](runner(e))


AR_BYTES = 204 * KB


class Arena:

    def __init__(self, nc):
        self.t = nc.alloc_sbuf_tensor("AR", [128, AR_BYTES // 2], BF16)

    def v(self, off_b, nbytes, dt=BF16, parts=(0, 128)):
        assert off_b % 4 == 0 and off_b + nbytes <= AR_BYTES, (off_b, nbytes)
        ap = self.t[:, off_b // 2:(off_b + nbytes) // 2]
        if dt != BF16:
            ap = ap.bitcast(dt)
        if parts != (0, 128):
            ap = ap[parts[0]:parts[1], :]
        return ap


def build(debug=False, race_detect=True, upto="F"):
    nc = bass.Bass("TRN2", target_bir_lowering=False,
                   detect_race_conditions=race_detect)
    d = {}

    def din(name, shape, dt):
        d[name] = nc.dram_tensor(name, list(shape), dt, kind="ExternalInput").ap()

    def dout(name, shape, dt):
        d[name] = nc.dram_tensor(name, list(shape), dt, kind="ExternalOutput").ap()

    din("xT", (4096, 4096), F32)
    din("xoT", (4096, 1024), F32)
    din("wkvaT", (4096, 320), F32)
    din("wqaT", (4096, 1024), F32)
    din("wqbT", (1024, 4096), F32)
    din("wkvbT", (256, 6144), F32)
    din("wo_sta", (32, 128, 32, 128), F32)
    din("A_sta", (32, 128, 512), F32)
    din("B_sta", (4, 128, 4096), F32)
    din("WqT", (32, 128, 16), F32)
    din("keysT", (16, 64), F32)
    din("lnw_q", (128, 8), F32)
    din("lnw_kv", (128, 2), F32)
    din("cosF", (64, 4096), F32)
    din("sinF", (64, 4096), F32)
    din("cos_o", (64, 1024), F32)
    din("sin_o", (64, 1024), F32)
    din("maskPT", (256, 1024), BF16)
    din("ident16", (128, 128), BF16)
    din("ident32", (128, 128), F32)
    din("rot", (64, 64), BF16)
    din("ones_col32", (128, 1), F32)
    din("ones_col16", (128, 1), BF16)
    din("ones_row16", (1, 512), BF16)
    din("ones_row32", (1, 128), F32)
    din("cvec", (128, 8), F32)
    din("R_exp", (64, 512), F32)
    dout("out", (4096, 1024), F32)
    if debug:
        dout("dbg_cnorm", (256, 4096), BF16)
        dout("dbg_kpe", (64, 4096), BF16)
        dout("dbg_qa", (128, 8192), BF16)
        dout("dbg_q", (128, H * 1024), BF16)
        dout("dbg_attn", (128, H * 1024), BF16)
        dout("dbg_x1", (128, 32 * 1024), BF16)
        dout("dbg_gate", (128, 512), F32)
        dout("dbg_stm", (128, 512), F32)
        dout("dbg_zg", (128, 4096), BF16)
        dout("dbg_x2p", (128, 32 * 1024), BF16)
        dout("dbg_kf", (256, 4096), BF16)
        dout("dbg_vx", (256, 32 * VSTRIDE), BF16)
        dout("dbg_acc", (256, NSLOT * VSTRIDE), F32)
        dout("dbg_P", (256, 1024), BF16)
        dout("dbg_mr", (128, NSLOT), F32)
        dout("dbg_it", (128, 8192), F32)

    with ExitStack() as ctx:
        em = Em(nc, ctx)
        _LAST_EM[0] = em
        _emit(nc, em, d, ctx, debug, upto)
        _replay(nc, em, ctx)
    return nc


def _emit(nc, em, d, ctx, debug, upto="F"):
    def _finish():
        for grp, n in em.ndma.items():
            em.op("sp", "wait_ge", em.sem_dma[grp], 16 * n)

    ar = Arena(nc)

    def sbt(name, shape, dt):
        return ctx.enter_context(nc.sbuf_tensor(name, list(shape), dt))

    consts = sbt("consts", (128, 8), F32)
    SCALE = consts[:, 0:1]
    NEGH = consts[:, 1:2]
    idn16 = sbt("idn16", (128, 128), BF16)
    idn32 = sbt("idn32", (128, 128), F32)
    rot_sb = sbt("rot_sb", (128, 64), BF16)
    onec32 = sbt("onec32", (128, 1), F32)
    onec16 = sbt("onec16", (128, 1), BF16)
    oner16 = sbt("oner16", (1, 512), BF16)
    oner32 = sbt("oner32", (1, 128), F32)
    lnwq_sb = sbt("lnwq_sb", (128, 8), F32)
    lnwkv_sb = sbt("lnwkv_sb", (128, 2), F32)

    mask2 = ar.v(0, 4 * KB)
    cos_o_sb = ar.v(4 * KB, 2 * KB, BF16, (64, 128))
    sin_o_sb = ar.v(6 * KB, 2 * KB, BF16, (64, 128))
    woslab = [ar.v(8 * KB + i * 8 * KB, 8 * KB) for i in range(2)]
    cnorm = [ar.v(24 * KB + c * 8 * KB, 8 * KB) for c in range(2)]
    kpe = ar.v(40 * KB, 8 * KB, BF16, (0, 64))
    Q = ar.v(48 * KB, 64 * KB)

    tcs = None
    for name, apx in [("cvec", consts[:]), ("ident16", idn16[:]), ("ident32", idn32[:]),
                      ("ones_col32", onec32[:]), ("ones_col16", onec16[:]),
                      ("ones_row16", oner16[:]), ("ones_row32", oner32[:]),
                      ("lnw_q", lnwq_sb[:]), ("lnw_kv", lnwkv_sb[:])]:
        tcs = em.dma("sp", "gconst", apx, d[name])
    em.dma("sp", "gconst", rot_sb[0:64, :], d["rot"])
    em.dma("sp", "gconst", rot_sb[64:128, :], d["rot"])
    em.dma("sp", "gconst", mask2[:, 0:1024], d["maskPT"][0:128, :])
    tcs = em.dma("sp", "gconst", mask2[:, 1024:2048], d["maskPT"][128:256, :])
    em.dma("pool", "gcoso", cos_o_sb, d["cos_o"])
    t_coso = em.dma("pool", "gcoso", sin_o_sb, d["sin_o"])
    em.preamble(tcs)

    xfm = [ar.v(112 * KB + i * 32 * KB, 32 * KB) for i in range(2)]
    wkva_sb = ar.v(176 * KB, 20 * KB)
    cosF_sb = ar.v(196 * KB, 8 * KB, BF16, (0, 64))
    sinF_sb = ar.v(60 * KB, 8 * KB, BF16, (0, 64))
    pescrA = ar.v(48 * KB, 1 * KB, BF16, (0, 64))
    t1A = ar.v(50 * KB, 2 * KB, F32, (0, 64))
    t2A = ar.v(52 * KB, 1 * KB, BF16, (0, 64))
    rstdA = ar.v(54 * KB, 2 * KB, F32)
    sqA = ar.v(56 * KB, 2 * KB, F32)
    rowA = ar.v(58 * KB, 2 * KB, F32, (0, 1))

    with ExitStack() as pctx:
        psl = lambda n, s, t: pctx.enter_context(nc.psum_tensor(n, list(s), t))
        psL = [psl(f"psL{i}", (128, 512), F32) for i in range(3)]
        psS = psl("psS", (1, 512), F32)
        psB = psl("psB", (128, 512), F32)
        psR = psl("psR", (64, 512), F32)

        em.dma("pool", "gcosf", cosF_sb, d["cosF"])
        t_cosF = em.dma("pool", "gcosf", sinF_sb, d["sinF"])
        wkva_ap = d["wkvaT"].rearrange("(kc p) w -> kc p w", p=128).transpose([1, 0, 2])
        wkva_dst = wkva_sb.rearrange("p (kc w) -> p kc w", w=320)
        t_wkva = None
        for i in range(4):
            t_wkva = em.dma("pool", "gwkva", wkva_dst[:, 8 * i:8 * i + 8, :],
                            wkva_ap[:, 8 * i:8 * i + 8, :])
        def load_xfm(g):
            gp = g % 2
            tk = None
            for fc in range(32):
                tk = em.dma("pool", f"gxfm{gp}", xfm[gp][:, fc * 512:(fc + 1) * 512],
                            d["xT"][fc * 128:(fc + 1) * 128, g * 512:(g + 1) * 512])
            return tk

        x_tok = [None] * 8
        x_tok[0] = load_xfm(0)
        x_tok[1] = load_xfm(1)

        mm_mark = [None] * 8
        sq_done = [None] * 8
        dve_done = [None] * 8
        for g in range(8):
            gp = g % 2
            gc = slice(g * 512, (g + 1) * 512)
            em.waits("pe", [x_tok[g], t_wkva])
            if g >= 1:
                em.waits("pe", [dve_done[g - 1], sq_done[g - 1]])
            for ofc in range(3):
                mw = 128 if ofc < 2 else 64
                for kc in range(32):
                    em.opw("pe", [t_wkva], "matmul", psL[ofc][0:mw, :],
                           wkva_sb[:, kc * 320 + ofc * 128: kc * 320 + ofc * 128 + mw],
                           xfm[gp][:, kc * 512:(kc + 1) * 512],
                           start=(kc == 0), stop=(kc == 31))
            t_mm = em.mark("pe")
            mm_mark[g] = t_mm
            if g + 2 < 8:
                em.wait("pool", t_mm)
                x_tok[g + 2] = load_xfm(g + 2)

            em.wait("act", t_mm)
            em.op("act", "square", sqA, psL[0][:])
            t_sq0 = em.mark("act")
            em.wait("pe", t_sq0)
            em.op("pe", "matmul", psS[:], onec32[:], sqA, start=True, stop=False)
            t_s0 = em.mark("pe")
            em.wait("act", t_s0)
            em.op("act", "square", sqA, psL[1][:])
            em.op("act", "copy", pescrA, psL[2][0:64, :])
            t_sq1 = em.mark("act")
            sq_done[g] = t_sq1
            em.wait("pe", t_sq1)
            em.op("pe", "matmul", psS[:], onec32[:], sqA, start=False, stop=True)
            t_ssq = em.mark("pe")
            em.wait("dve", t_ssq)
            em.op("dve", "tensor_scalar", rowA, psS[:], 1.0 / KVR, EPS_RMS,
                  op0=ALU.mult, op1=ALU.add)
            t_var = em.mark("dve")
            em.wait("act", t_var)
            em.op("act", "activation", rowA, rowA, AF.Ln)
            em.self_edge("act")
            em.op("act", "activation", rowA, rowA, AF.Exp, scale=NEGH[0:1, :])
            t_rstd = em.mark("act")
            em.wait("pe", t_rstd)
            em.op("pe", "matmul", psB[:], oner32[:, 0:128], rowA, start=True, stop=True)
            t_bc = em.mark("pe")
            em.wait("dve", t_bc)
            em.op("dve", "tensor_copy", rstdA, psB[:])
            em.self_edge("dve")
            for c in range(2):
                em.op("dve", "scalar_tensor_tensor", cnorm[c][:, gc], psL[c][:],
                      lnwkv_sb[:, c:c + 1], rstdA, op0=ALU.mult, op1=ALU.mult)
            em.waits("dve", [t_cosF, sq_done[g]])
            em.op("dve", "tensor_tensor", t1A, pescrA, cosF_sb[:, gc], op=ALU.mult)
            em.op("dve", "tensor_tensor", t2A, pescrA, sinF_sb[:, gc], op=ALU.mult)
            t_rope = em.mark("dve")
            em.wait("pe", t_rope)
            em.op("pe", "matmul", psR[:], rot_sb[0:64, :], t2A, start=True, stop=True)
            t_rot = em.mark("pe")
            em.wait("dve", t_rot)
            em.op("dve", "tensor_tensor", kpe[:, gc], t1A, psR[:], op=ALU.add)
            dve_done[g] = em.mark("dve")
        em.barrier()
        if debug:
            for c in range(2):
                em.dma("sp", "gdbg", d["dbg_cnorm"][c * 128:(c + 1) * 128, :], cnorm[c][:])
            em.dma("sp", "gdbg", d["dbg_kpe"], kpe)
        if upto == "A":
            _finish()
            return

    xo16 = ar.v(48 * KB, 64 * KB)
    wqa_sb = ar.v(112 * KB, 64 * KB)
    qa_sb = ar.v(176 * KB, 16 * KB)
    sqB = ar.v(192 * KB, 4 * KB, F32)
    rowB = ar.v(196 * KB, 4 * KB, F32, (0, 1))
    rstdB = ar.v(200 * KB, 2 * KB)

    with ExitStack() as pctx:
        psl = lambda n, s, t: pctx.enter_context(nc.psum_tensor(n, list(s), t))
        psB2 = [psl(f"psB2{i}", (128, 1024), F32) for i in range(2)]
        psSq = psl("psSq", (1, 1024), F32)
        psBc = psl("psBc", (128, 1024), F32)

        t_xo = None
        for fc in range(32):
            t_xo = em.dma("pool", "gxo", xo16[:, fc * 1024:(fc + 1) * 1024],
                          d["xoT"][fc * 128:(fc + 1) * 128, :])
        wqa_ap = d["wqaT"].rearrange("(kc p) m -> kc p m", p=128).transpose([1, 0, 2])
        wqa_dst = wqa_sb.rearrange("p (kc m) -> p kc m", m=1024)
        t_wqa = None
        for i in range(4):
            t_wqa = em.dma("pool", "gwqa", wqa_dst[:, 8 * i:8 * i + 8, :],
                           wqa_ap[:, 8 * i:8 * i + 8, :])

        ev_mark = [None] * 8
        sqm = [None] * 8
        for ofc in range(8):
            p = ofc % 2
            em.waits("pe", [t_xo, t_wqa])
            if ofc >= 2:
                em.waits("pe", [ev_mark[ofc - 2], sqm[ofc - 2]])
            for kc in range(32):
                for hh in range(2):
                    em.opw("pe", [t_wqa], "matmul",
                           psB2[p][:, hh * 512:(hh + 1) * 512],
                           wqa_sb[:, kc * 1024 + ofc * 128: kc * 1024 + ofc * 128 + 128],
                           xo16[:, kc * 1024 + hh * 512: kc * 1024 + hh * 512 + 512],
                           start=(kc == 0), stop=(kc == 31))
            t_mm = em.mark("pe")
            if upto != "B0":
                em.wait("act", t_mm)
                actf = "copy" if upto == "B0c" else "square"
                em.op("act", actf, sqB[:, 0:512], psB2[p][:, 0:512])
                em.op("act", actf, sqB[:, 512:1024], psB2[p][:, 512:1024])
                sqm[ofc] = em.mark("act")
                if upto not in ("B0a", "B0c"):
                    em.wait("pe", sqm[ofc])
                    for hh in range(2):
                        em.op("pe", "matmul", psSq[:, hh * 512:(hh + 1) * 512], onec32[:],
                              sqB[:, hh * 512:(hh + 1) * 512],
                              start=(ofc == 0), stop=(ofc == 7))
            else:
                sqm[ofc] = t_mm
            em.waits("dve", [t_mm, sqm[ofc]])
            em.op("dve", "tensor_scalar", qa_sb[:, ofc * 1024:(ofc + 1) * 1024],
                  psB2[p][:], lnwq_sb[:, ofc:ofc + 1], None, op0=ALU.mult)
            ev_mark[ofc] = em.mark("dve")
        t_sqq = em.mark("pe")
        if upto in ("B0", "B0a", "B0c"):
            if debug:
                em.dma("sp", "gdbg", d["dbg_qa"], qa_sb)
            _finish()
            return
        if upto == "B1":
            if debug:
                em.dma("sp", "gdbg", d["dbg_qa"], qa_sb)
            _finish()
            return
        em.wait("dve", t_sqq)
        em.op("dve", "tensor_scalar", rowB, psSq[:], 1.0 / 1024.0, EPS_RMS,
              op0=ALU.mult, op1=ALU.add)
        t_var = em.mark("dve")
        em.wait("act", t_var)
        em.op("act", "activation", rowB, rowB, AF.Ln)
        em.self_edge("act")
        em.op("act", "activation", rowB, rowB, AF.Exp, scale=NEGH[0:1, :])
        t_rstd = em.mark("act")
        em.wait("pe", t_rstd)
        for hh in range(2):
            em.op("pe", "matmul", psBc[:, hh * 512:(hh + 1) * 512], oner32[:, 0:128],
                  rowB[0:1, hh * 512:(hh + 1) * 512], start=True, stop=True)
        t_bc = em.mark("pe")
        em.wait("dve", t_bc)
        em.op("dve", "tensor_copy", rstdB, psBc[:])
        em.self_edge("dve")
        for ofc in range(8):
            em.op("dve", "tensor_tensor", qa_sb[:, ofc * 1024:(ofc + 1) * 1024],
                  qa_sb[:, ofc * 1024:(ofc + 1) * 1024], rstdB, op=ALU.mult)
        em.mark("dve")
        em.barrier()
        if debug:
            em.dma("sp", "gdbg", d["dbg_qa"], qa_sb)
        if upto == "B":
            _finish()
            return

    wqb_sb = ar.v(112 * KB, 64 * KB)
    pe_scr = ar.v(192 * KB, 2 * KB, BF16, (64, 128))
    a_scr = ar.v(194 * KB, 4 * KB, F32, (64, 128))
    b_scr = ar.v(198 * KB, 2 * KB, BF16, (64, 128))

    with ExitStack() as pctx:
        psl = lambda n, s, t: pctx.enter_context(nc.psum_tensor(n, list(s), t))
        psC = [psl(f"psC{i}", (128, 1024), F32) for i in range(2)]
        psRo = psl("psRo", (128, 1024), F32)

        wqb_ap = d["wqbT"].rearrange("(kc p) m -> kc p m", p=128).transpose([1, 0, 2])
        wqb_dst = wqb_sb.rearrange("p (kc m) -> p kc m", m=4096)
        t_wqb = None
        for i in range(8):
            t_wqb = em.dma("pool", "gwqb", wqb_dst[:, i:i + 1, :], wqb_ap[:, i:i + 1, :])

        cp_mark = [None] * H
        rope_mark = [None] * H
        q_mark = [None] * H
        for h in range(H):
            p = h % 2
            hc = slice(h * 1024, (h + 1) * 1024)
            em.waits("pe", [t_wqb])
            if h >= 2:
                em.wait("pe", cp_mark[h - 2])
            for kc in range(8):
                for hh in range(2):
                    em.opw("pe", [t_wqb], "matmul", psC[p][:, hh * 512:(hh + 1) * 512],
                           wqb_sb[:, kc * 4096 + h * 128: kc * 4096 + h * 128 + 128],
                           qa_sb[:, kc * 1024 + hh * 512: kc * 1024 + hh * 512 + 512],
                           start=(kc == 0), stop=(kc == 7))
            t_mm = em.mark("pe")
            em.wait("act", t_mm)
            if h >= 1:
                em.wait("act", rope_mark[h - 1])
            em.op("act", "copy", Q[0:64, hc], psC[p][0:64, :])
            em.op("act", "copy", pe_scr, psC[p][64:128, :])
            cp_mark[h] = em.mark("act")
            em.waits("pool", [cp_mark[h], t_coso])
            if h >= 1:
                em.waits("pool", [q_mark[h - 1], rope_mark[h - 1]])
            em.op("pool", "tensor_tensor", a_scr, pe_scr, cos_o_sb, op=ALU.mult)
            em.op("pool", "tensor_tensor", b_scr, pe_scr, sin_o_sb, op=ALU.mult)
            rope_mark[h] = em.mark("pool")
            em.wait("pe", rope_mark[h])
            if h >= 1:
                em.wait("pe", q_mark[h - 1])
            for hh in range(2):
                em.op("pe", "matmul", psRo[64:128, hh * 512:(hh + 1) * 512],
                      rot_sb[64:128, :], b_scr[:, hh * 512:(hh + 1) * 512],
                      start=True, stop=True, tile_position=(64, 64))
            t_rot = em.mark("pe")
            em.wait("dve", t_rot)
            em.op("dve", "tensor_tensor", Q[64:128, hc], a_scr, psRo[64:128, :],
                  op=ALU.add)
            q_mark[h] = em.mark("dve")
        em.barrier()
        if debug:
            em.dma("sp", "gdbg", d["dbg_q"], Q)
        if upto == "C":
            _finish()
            return

    wkvb_sb = [ar.v(112 * KB + c * 12 * KB, 12 * KB) for c in range(2)]
    Kf = [ar.v(136 * KB + i * 8 * KB, 8 * KB) for i in range(2)]
    Vx = [ar.v(152 * KB + i * 8448, 8448) for i in range(2)]
    P_sb = [ar.v(169 * KB + i * 2 * KB, 2 * KB) for i in range(2)]
    PT_sb = [ar.v(173 * KB + i * 2 * KB, 2 * KB) for i in range(2)]
    accT = [ar.v(177 * KB + i * 4224, 4224, F32) for i in range(2)]
    m_run = ar.v(186 * KB, 32, F32)
    mnb = [ar.v(186 * KB + 64 + 32 * i, 16) for i in range(2)]
    mx_c = ar.v(186 * KB + 128, 4, F32)
    f_c = [ar.v(186 * KB + 136 + 4 * i, 4, F32) for i in range(2)]
    diff8 = ar.v(186 * KB + 160, 32, F32)
    mrow = [ar.v(186 * KB + 256 + 256 * i, 256, BF16, (0, 1)) for i in range(2)]
    att_tm = ar.v(187 * KB, 256)
    rcp_c = ar.v(187 * KB + 256, 4, F32)

    dbg_it_sb = ar.v(188 * KB, 16 * KB, F32)
    t_wo01 = None
    wo_src = d["wo_sta"].rearrange("o p hc c -> o p (hc c)")
    with ExitStack() as pctx:
        psl = lambda n, s, t: pctx.enter_context(nc.psum_tensor(n, list(s), t))
        sA = psl("sA", (128, 1024), F32)
        sB = psl("sB", (128, 1024), F32)
        ptP = [psl(f"ptP{i}", (128, 1024), BF16) for i in range(2)]
        mix = psl("mix", (128, 512), F32)
        mix2 = psl("mix2", (128, 512), F32)
        pv_ = [mix[:, 0:129], mix2[:, 0:129]]
        mixb = mix[:].bitcast(BF16)
        atp = mixb[:, 640:768]
        scores = [sA, sB]

        t_wkvb = None
        for c in range(2):
            t_wkvb = em.dma("pool", "gwkvb", wkvb_sb[c][:], d["wkvbT"][c * 128:(c + 1) * 128, :])
        for i in range(2):
            em.op("dve", "memset",
                  Vx[i].rearrange("p (s w) -> p s w", w=VSTRIDE)[:, :, 128:129], 1.0)
            em.op("pool", "tensor_copy", Kf[i][64:128, :], kpe)
        t_init = em.mark("dve")
        t_init_p = em.mark("pool")

        t_wo0 = em.dma("pool", "gwo0", woslab[0][:], wo_src[0])
        t_wo1 = em.dma("pool", "gwo1", woslab[1][:], wo_src[1])
        t_wo01 = (t_wo0, t_wo1)

        iters = []
        first_it = {}
        for h in range(H):
            for s in range(NSLOT):
                for kb in range(TRIPS[s]):
                    if s == 0 and kb == 0:
                        first_it[h] = len(iters)
                    iters.append(dict(h=h, s=s, kb=kb, first=(kb == 0),
                                      last=(kb == TRIPS[s] - 1)))
        N = len(iters)

        tok_qk = [None] * (N + 2)
        tok_exp = [None] * N
        tok_pvt = [None] * (N + 1)
        tok_evac = [None] * N
        fin_done = [None, None]
        tok_flash_done = {}
        tok_exp_ready = {}
        fin_pe_prev = [None]
        fin_act_prev = [None]

        def emit_expansion(h, bank):
            hp = h % 2
            war = tok_flash_done.get(h - 2)
            ev = []
            kn_ev = [None] * 8
            v_ev = [None] * 8
            em.waits("pe", [t_wkvb])
            for b in range(8):
                if b >= 1:
                    em.wait("pe", kn_ev[b - 1])
                for c in range(2):
                    em.opw("pe", [t_wkvb], "matmul", bank[0:64, 0:512],
                           wkvb_sb[c][:, h * 192: h * 192 + 64],
                           cnorm[c][:, b * 512:(b + 1) * 512],
                           start=(c == 0), stop=(c == 1))
                t_kn = em.mark("pe")
                em.waits("dve", [t_kn, war, t_init, t_init_p])
                em.op("dve", "tensor_copy", Kf[hp][0:64, b * 512:(b + 1) * 512],
                      bank[0:64, 0:512])
                kn_ev[b] = em.mark("dve")
                ev.append(kn_ev[b])
                if b >= 1:
                    em.wait("pe", v_ev[b - 1])
                for sub in range(4):
                    kk = b * 4 + sub
                    for c in range(2):
                        em.opw("pe", [t_wkvb], "matmul",
                               bank[:, 512 + sub * 128: 512 + sub * 128 + 128],
                               cnorm[c][:, kk * 128:(kk + 1) * 128],
                               wkvb_sb[c][:, h * 192 + 64: h * 192 + 192],
                               start=(c == 0), stop=(c == 1))
                t_v = em.mark("pe")
                em.waits("act", [t_v, war, t_init, t_init_p])
                dst = Vx[hp][:, b * 4 * VSTRIDE:(b + 1) * 4 * VSTRIDE]
                dst = dst.rearrange("p (s w) -> p s w", w=VSTRIDE)[:, :, 0:128]
                em.op("act", "copy", dst,
                      bank[:, 512:1024].rearrange("p (s w) -> p s w", w=128))
                v_ev[b] = em.mark("act")
                ev.append(v_ev[b])
            tok_exp_ready[h] = ev

        def emit_qk(it):
            cu = iters[it]
            par = it % 2
            hp = cu["h"] % 2
            if it - 2 >= 0:
                em.wait("pe", tok_exp[it - 2])
            em.waits("pe", tok_exp_ready.get(cu["h"], []))
            kb = cu["kb"]
            lhsQ = Q[:, cu["h"] * 1024 + cu["s"] * 128: cu["h"] * 1024 + cu["s"] * 128 + 128]
            for hh in range(2):
                em.op("pe", "matmul", scores[par][:, hh * 512:(hh + 1) * 512],
                      lhsQ, Kf[hp][:, kb * 1024 + hh * 512: kb * 1024 + hh * 512 + 512],
                      start=True, stop=True)
            tok_qk[it] = em.mark("pe")

        emit_expansion(0, scores[0])
        emit_qk(0)

        for r in range(N + 1):
            prev = iters[r - 1] if r >= 1 else None
            cur = iters[r] if r < N else None
            nxt = iters[r + 1] if r + 1 < N else None

            if cur is not None:
                par = r % 2
                em.wait("act", tok_qk[r])
                if r - 2 >= 0:
                    em.wait("act", tok_pvt[r - 2])
                em.op("act", "activation", P_sb[par][:], scores[par][:], AF.Exp,
                      scale=SCALE)
                tok_exp[r] = em.mark("act")

            boundary = (nxt is not None and cur is not None
                        and nxt["h"] != cur["h"])
            if nxt is not None and not boundary:
                emit_qk(r + 1)

            if prev is not None:
                parp = (r - 1) % 2
                sp = prev["s"] % 2
                em.wait("pe", tok_evac[r - 1])
                if prev["first"] and fin_done[sp] is not None:
                    em.wait("pe", fin_done[sp])
                em.waits("pe", [fin_act_prev[0]])
                hpp = prev["h"] % 2
                kbp = prev["kb"]
                for t in range(8):
                    em.opw("pe", [tok_evac[r - 1]], "matmul", pv_[sp],
                           PT_sb[parp][:, t * 128:(t + 1) * 128],
                           Vx[hpp][:, (kbp * 8 + t) * VSTRIDE:
                                     (kbp * 8 + t) * VSTRIDE + 129],
                           start=(prev["first"] and t == 0),
                           stop=(prev["last"] and t == 7))

            if cur is not None:
                em.wait("pe", tok_exp[r])
                if r - 2 >= 0:
                    em.wait("pe", tok_evac[r - 2])
                for t in range(8):
                    em.opw("pe", [tok_exp[r]], "transpose",
                           ptP[par][:, t * 128:(t + 1) * 128],
                           P_sb[par][:, t * 128:(t + 1) * 128], idn16[:])
            tok_pvt[r] = em.mark("pe")

            if prev is not None and prev["last"]:
                sp = prev["s"] % 2
                h_, sf = prev["h"], prev["s"]
                em.wait("dve", tok_pvt[r])
                if fin_pe_prev[0] is not None:
                    em.wait("dve", fin_pe_prev[0])
                em.op("dve", "reciprocal", rcp_c, pv_[sp][:, 128:129])
                em.self_edge("dve")
                em.op("dve", "tensor_tensor", att_tm, pv_[sp][:, 0:128],
                      rcp_c.broadcast_to((128, 128)), op=ALU.mult)
                t_fin = em.mark("dve")
                fin_done[sp] = t_fin
                em.waits("pe", [t_fin, fin_act_prev[0]])
                em.opw("pe", [t_fin], "transpose", atp, att_tm, idn16[:])
                t_atp = em.mark("pe")
                fin_pe_prev[0] = t_atp
                em.wait("dve", t_atp)
                em.op("dve", "tensor_copy",
                      Q[:, h_ * 1024 + sf * 128: h_ * 1024 + sf * 128 + 128],
                      atp)
                fin_act_prev[0] = em.mark("dve")
                if sf == NSLOT - 1:
                    tok_flash_done[h_] = fin_act_prev[0]

            if cur is not None:
                e = "dve" if (cur["last"] or r % 2 == 0) else "act"
                em.wait(e, tok_pvt[r])
                if cur["last"]:
                    fam = 0 if cur["s"] < 4 else 1
                    em.op(e, "tensor_tensor", PT_sb[par][:], ptP[par][:],
                          mask2[:, fam * 1024:(fam + 1) * 1024], op=ALU.mult)
                elif e == "act":
                    em.op(e, "copy", PT_sb[par][:], ptP[par][:])
                else:
                    em.op(e, "tensor_copy", PT_sb[par][:], ptP[par][:])
                tok_evac[r] = em.mark(e)

            if boundary:
                if r - 1 >= 0:
                    em.wait("pe", tok_exp[r - 1])
                emit_expansion(nxt["h"], scores[(r + 1) % 2])
                emit_qk(r + 1)

        em.barrier()
        if debug:
            em.dma("sp", "gdbg", d["dbg_attn"], Q)
            for i in range(2):
                em.dma("sp", "gdbg", d["dbg_kf"][i * 128:(i + 1) * 128, :], Kf[i][:])
                em.dma("sp", "gdbg", d["dbg_vx"][i * 128:(i + 1) * 128, :], Vx[i][:])
                em.dma("sp", "gdbg", d["dbg_acc"][i * 128:(i + 1) * 128, :], accT[i][:])
                em.dma("sp", "gdbg", d["dbg_P"][i * 128:(i + 1) * 128, :], P_sb[i][:])
            em.dma("sp", "gdbg", d["dbg_mr"], m_run)
            em.dma("sp", "gdbg", d["dbg_it"][:, 0:4096], dbg_it_sb[:, 0:4096])
        if upto == "D":
            _finish()
            return

    Xw = ar.v(112 * KB, 64 * KB)
    xres = [ar.v(176 * KB + i * 4 * KB, 4 * KB, F32) for i in range(2)]
    sqE = ar.v(184 * KB, 2 * KB)
    meanB = ar.v(186 * KB, 2 * KB)
    rstdB_ln = ar.v(188 * KB, 2 * KB)
    rowE = ar.v(0, 4 * KB, F32, (0, 1))
    rowE2 = ar.v(4 * KB, 4 * KB, F32, (0, 1))
    rowE3 = ar.v(190 * KB, 4 * KB, F32, (0, 1))
    a_spots = ([24 * KB + i * KB for i in range(24)]
               + [194 * KB + i * KB for i in range(8)])
    A_ch = [ar.v(a_spots[kc], KB) for kc in range(32)]
    Wq_sb = ar.v(202 * KB, KB)
    keys_sb = ar.v(203 * KB, 128, BF16, (0, 16))

    with ExitStack() as pctx:
        psl = lambda n, s, t: pctx.enter_context(nc.psum_tensor(n, list(s), t))
        psE = [psl(f"psE{i}", (128, 1024), F32) for i in range(2)]
        psS1 = psl("psS1", (1, 1024), F32)
        psS2 = psl("psS2", (1, 1024), F32)

        t_wo = [None] * 34
        t_wo[0], t_wo[1] = t_wo01
        t_xr = [None] * 34
        t_xr[0] = em.dma("sp", "gxr0", xres[0][:], d["xoT"][0:128, :])
        t_xr[1] = em.dma("sp", "gxr1", xres[1][:], d["xoT"][128:256, :])
        A_src = d["A_sta"].transpose([1, 0, 2])
        for kc in range(32):
            em.dma("pool", "gfa", A_ch[kc], A_src[:, kc:kc + 1, :])
        em.dma("pool", "gfa", Wq_sb.rearrange("p (kc m) -> p kc m", m=16),
               d["WqT"].transpose([1, 0, 2]))
        t_fw = em.dma("pool", "gfa", keys_sb, d["keysT"])

        ev_mark = [None] * 32
        mm_mark = [None] * 32
        st_mark = [None] * 32
        for ofc in range(32):
            p = ofc % 2
            em.waits("pe", [t_wo[ofc], t_xr[ofc]])
            if ofc >= 2:
                em.wait("pe", ev_mark[ofc - 2])
            for hc in range(32):
                for hh in range(2):
                    em.opw("pe", [t_wo[ofc]], "matmul",
                           psE[p][:, hh * 512:(hh + 1) * 512],
                           woslab[p][:, hc * 128:(hc + 1) * 128],
                           Q[:, hc * 1024 + hh * 512: hc * 1024 + hh * 512 + 512],
                           start=(hc == 0), stop=(hc == 31))
            mm_mark[ofc] = em.mark("pe")
            em.waits("dve", [mm_mark[ofc], t_xr[ofc]])
            em.op("dve", "tensor_tensor", Xw[:, ofc * 1024:(ofc + 1) * 1024],
                  psE[p][:], xres[p][:], op=ALU.add)
            ev_mark[ofc] = em.mark("dve")
            if ofc + 2 < 32:
                em.wait("pool", mm_mark[ofc])
                t_wo[ofc + 2] = em.dma("pool", f"gwo{p}", woslab[p][:], wo_src[ofc + 2])
                em.wait("sp", ev_mark[ofc])
                t_xr[ofc + 2] = em.dma("sp", f"gxr{p}", xres[p][:],
                                       d["xoT"][(ofc + 2) * 128:(ofc + 3) * 128, :])
            em.wait("act", ev_mark[ofc])
            if ofc >= 1:
                em.wait("act", st_mark[ofc - 1])
            em.op("act", "square", sqE, Xw[:, ofc * 1024:(ofc + 1) * 1024])
            t_sq = em.mark("act")
            em.waits("pe", [ev_mark[ofc], t_sq])
            for hh in range(2):
                em.op("pe", "matmul", psS1[:, hh * 512:(hh + 1) * 512], onec16[:],
                      Xw[:, ofc * 1024 + hh * 512: ofc * 1024 + hh * 512 + 512],
                      start=(ofc == 0), stop=(ofc == 31))
                em.op("pe", "matmul", psS2[:, hh * 512:(hh + 1) * 512], onec16[:],
                      sqE[:, hh * 512:(hh + 1) * 512],
                      start=(ofc == 0), stop=(ofc == 31))
            st_mark[ofc] = em.mark("pe")
        em.wait("dve", st_mark[31])
        em.op("dve", "tensor_scalar", rowE, psS1[:], 1.0 / DM, None, op0=ALU.mult)
        em.op("dve", "tensor_scalar", rowE2, psS2[:], 1.0 / DM, None, op0=ALU.mult)
        em.self_edge("dve")
        em.op("dve", "tensor_tensor", rowE3, rowE, rowE, op=ALU.mult)
        em.self_edge("dve")
        em.op("dve", "tensor_tensor", rowE2, rowE2, rowE3, op=ALU.subtract)
        em.self_edge("dve")
        em.op("dve", "tensor_scalar", rowE2, rowE2, EPS_LN, None, op0=ALU.add)
        t_var = em.mark("dve")
        em.wait("act", t_var)
        em.op("act", "activation", rowE2, rowE2, AF.Ln)
        em.self_edge("act")
        em.op("act", "activation", rowE2, rowE2, AF.Exp, scale=NEGH[0:1, :])
        t_rstd = em.mark("act")
        em.waits("pe", [t_rstd, ev_mark[31], ev_mark[30]])
        for hh in range(2):
            em.op("pe", "matmul", psE[0][:, hh * 512:(hh + 1) * 512], oner32[:, 0:128],
                  rowE[0:1, hh * 512:(hh + 1) * 512], start=True, stop=True)
            em.op("pe", "matmul", psE[1][:, hh * 512:(hh + 1) * 512], oner32[:, 0:128],
                  rowE2[0:1, hh * 512:(hh + 1) * 512], start=True, stop=True)
        t_bc = em.mark("pe")
        em.wait("dve", t_bc)
        em.op("dve", "tensor_copy", meanB, psE[0][:])
        em.op("dve", "tensor_copy", rstdB_ln, psE[1][:])
        em.self_edge("dve")
        for ofc in range(32):
            em.op("dve", "tensor_tensor", Xw[:, ofc * 1024:(ofc + 1) * 1024],
                  Xw[:, ofc * 1024:(ofc + 1) * 1024], meanB, op=ALU.subtract)
        em.self_edge("dve")
        for ofc in range(32):
            em.op("dve", "tensor_tensor", Xw[:, ofc * 1024:(ofc + 1) * 1024],
                  Xw[:, ofc * 1024:(ofc + 1) * 1024], rstdB_ln, op=ALU.mult)
        em.mark("dve")
        em.barrier()
        if debug:
            em.dma("sp", "gdbg", d["dbg_x1"], Xw)
        if upto == "E":
            _finish()
            return

    B_sb = ar.v(48 * KB, 32 * KB)
    zg = [ar.v(80 * KB + i * 2 * KB, 2 * KB) for i in range(4)]
    G_sb = [ar.v(88 * KB + i * 4 * KB, 4 * KB, F32) for i in range(4)]
    qr_sb = ar.v(104 * KB, 2 * KB, BF16, (0, 16))
    gfm = ar.v(106 * KB, 4 * KB, F32, (0, 64))
    R_sb = ar.v(110 * KB, 2 * KB, F32, (0, 64))
    s_tm = ar.v(176 * KB, 2 * KB, F32)
    gate_tm = ar.v(178 * KB, 2 * KB, F32)
    swork = ar.v(180 * KB, 2 * KB, F32)
    gwork = ar.v(182 * KB, 2 * KB, F32)
    ework = ar.v(184 * KB, 2 * KB, F32)
    cur_c = [ar.v(186 * KB + 32 * i, 32, F32) for i in range(2)]
    m1_c = ar.v(186 * KB + 64, 32, F32)
    z_c = ar.v(186 * KB + 96, 32, F32)
    z_c2 = ar.v(186 * KB + 128, 32, F32)
    scf64 = ar.v(187 * KB, 512, F32, (0, 64))
    meanB2 = ar.v(104 * KB, 4 * KB, F32)
    rstdB2 = ar.v(108 * KB, 4 * KB, F32)
    c32 = [ar.v(8 * KB + i * 4 * KB, 4 * KB, F32) for i in range(2)]
    out_sb = [ar.v(16 * KB + i * 4 * KB, 4 * KB, F32) for i in range(2)]

    B_src = d["B_sta"].transpose([1, 0, 2])
    B_dst = B_sb.rearrange("p (kc m) -> p kc m", m=4096)
    t_B = None
    for i in range(4):
        t_B = em.dma("pool", "gb", B_dst[:, i:i + 1, :], B_src[:, i:i + 1, :])
    t_R = em.dma("sp", "gr", R_sb, d["R_exp"])

    with ExitStack() as pctx:
        psl = lambda n, s, t: pctx.enter_context(nc.psum_tensor(n, list(s), t))
        psQR = psl("psQR", (16, 1024), F32)
        psSc = psl("psSc", (64, 1024), F32)
        psT2 = psl("psT2", (128, 64), F32)
        psT3 = psl("psT3", (64, 128), F32)
        psG = psl("psG", (128, 1024), F32)

        em.wait("pe", t_fw)
        for kc in range(32):
            for hh in range(2):
                em.op("pe", "matmul", psQR[0:16, hh * 512:(hh + 1) * 512],
                      Wq_sb[:, kc * 16:(kc + 1) * 16],
                      Xw[:, kc * 1024 + hh * 512: kc * 1024 + hh * 512 + 512],
                      start=(kc == 0), stop=(kc == 31))
        t_qr = em.mark("pe")
        em.wait("dve", t_qr)
        em.op("dve", "tensor_copy", qr_sb, psQR[:])
        t_qre = em.mark("dve")
        em.wait("pe", t_qre)
        for hh in range(2):
            em.opw("pe", [t_fw], "matmul", psSc[0:64, hh * 512:(hh + 1) * 512],
                   keys_sb, qr_sb[:, hh * 512:(hh + 1) * 512],
                   start=True, stop=True)
        t_sc = em.mark("pe")

        tp_mark = [None] * 8
        sev_mark = [None] * 8
        for t in range(8):
            em.wait("dve", t_sc)
            em.op("dve", "tensor_copy", scf64, psSc[0:64, t * 128:(t + 1) * 128])
            t_ev = em.mark("dve")
            em.wait("pe", t_ev)
            if t >= 1:
                em.wait("pe", sev_mark[t - 1])
            em.opw("pe", [t_ev], "transpose", psT2[:], scf64, idn32[0:64, 0:64])
            tp_mark[t] = em.mark("pe")
            em.wait("dve", tp_mark[t])
            em.op("dve", "tensor_copy", s_tm[:, t * 64:(t + 1) * 64], psT2[:])
            sev_mark[t] = em.mark("dve")

        def r3(apx):
            return apx.rearrange("p (t e) -> p t e", e=64)

        def bc4(col):
            return col.rearrange("p t -> p t ()").broadcast_to((128, 4, 64))

        SL = [slice(0, 256), slice(256, 512)]
        CS = [slice(0, 4), slice(4, 8)]
        em.self_edge("dve")
        em.op("dve", "tensor_copy", swork, s_tm)
        for hv in (0, 1):
            em.op("dve", "reduce_max", m1_c[:, CS[hv]], r3(s_tm[:, SL[hv]]),
                  axis=AX)
        for i in range(15):
            pc = m1_c if i == 0 else cur_c[(i - 1) % 2]
            em.self_edge("dve")
            for hv in (0, 1):
                em.op("dve", "tensor_tensor", r3(gwork[:, SL[hv]]),
                      r3(swork[:, SL[hv]]), bc4(pc[:, CS[hv]]), op=ALU.is_ge)
            em.self_edge("dve")
            for hv in (0, 1):
                em.op("dve", "scalar_tensor_tensor", swork[:, SL[hv]],
                      gwork[:, SL[hv]], -1e30, swork[:, SL[hv]],
                      op0=ALU.mult, op1=ALU.add)
            em.self_edge("dve")
            for hv in (0, 1):
                em.op("dve", "reduce_max", cur_c[i % 2][:, CS[hv]],
                      r3(swork[:, SL[hv]]), axis=AX)
        thr = cur_c[14 % 2]
        em.self_edge("dve")
        for hv in (0, 1):
            em.op("dve", "tensor_tensor", r3(ework[:, SL[hv]]),
                  r3(s_tm[:, SL[hv]]), bc4(m1_c[:, CS[hv]]), op=ALU.subtract)
        t_sh = em.mark("dve")
        em.wait("act", t_sh)
        em.op("act", "activation", ework, ework, AF.Exp)
        t_e = em.mark("act")
        em.wait("dve", t_e)
        for hv in (0, 1):
            em.op("dve", "tensor_tensor", r3(gwork[:, SL[hv]]),
                  r3(s_tm[:, SL[hv]]), bc4(thr[:, CS[hv]]), op=ALU.is_ge)
        em.self_edge("dve")
        for hv in (0, 1):
            em.op("dve", "tensor_tensor", ework[:, SL[hv]], ework[:, SL[hv]],
                  gwork[:, SL[hv]], op=ALU.mult)
        em.self_edge("dve")
        for hv in (0, 1):
            em.op("dve", "reduce_sum", z_c[:, CS[hv]], r3(ework[:, SL[hv]]),
                  axis=AX)
        em.self_edge("dve")
        em.op("dve", "reciprocal", z_c2, z_c)
        em.self_edge("dve")
        for hv in (0, 1):
            em.op("dve", "tensor_tensor",
                  r3(gate_tm[:, SL[hv]]), r3(ework[:, SL[hv]]),
                  bc4(z_c2[:, CS[hv]]), op=ALU.mult)
        gt_all = em.mark("dve")

        gfm_ev = [None] * 8
        for t in range(8):
            em.wait("pe", gt_all)
            if t >= 1:
                em.wait("pe", gfm_ev[t - 1])
            em.opw("pe", [gt_all], "transpose", psT3[:],
                   gate_tm[:, t * 64:(t + 1) * 64], idn32[:])
            t_tp = em.mark("pe")
            em.wait("dve", t_tp)
            em.op("dve", "tensor_copy", gfm[:, t * 128:(t + 1) * 128], psT3[:])
            gfm_ev[t] = em.mark("dve")
        em.waits("pe", gfm_ev)
        em.wait("pe", t_R)
        g_ev = [None] * 4
        for c in range(4):
            if c >= 1:
                em.wait("pe", g_ev[c - 1])
            for hh in range(2):
                em.opw("pe", [t_R], "matmul", psG[:, hh * 512:(hh + 1) * 512],
                       R_sb[:, c * 128:(c + 1) * 128],
                       gfm[:, hh * 512:(hh + 1) * 512], start=True, stop=True)
            t_g = em.mark("pe")
            em.wait("dve", t_g)
            em.op("dve", "tensor_copy", G_sb[c][:], psG[:])
            g_ev[c] = em.mark("dve")
        em.barrier()

    with ExitStack() as pctx:
        psl = lambda n, s, t: pctx.enter_context(nc.psum_tensor(n, list(s), t))
        psZ = [psl(f"psZ{i}", (128, 1024), F32) for i in range(2)]
        psS1b = psl("psS1b", (1, 1024), F32)
        psS2b = psl("psS2b", (1, 1024), F32)

        if debug:
            em.dma("sp", "gdbg", d["dbg_gate"], gate_tm)
            em.dma("sp", "gdbg", d["dbg_stm"], s_tm)
        zg_mark = [None] * 4
        for nc4 in range(4):
            p = nc4 % 2
            if nc4 >= 2:
                em.wait("pe", zg_mark[nc4 - 2])
            for kc in range(32):
                for hh in range(2):
                    em.opw("pe", [t_fw], "matmul", psZ[p][:, hh * 512:(hh + 1) * 512],
                           A_ch[kc][:, nc4 * 128:(nc4 + 1) * 128],
                           Xw[:, kc * 1024 + hh * 512: kc * 1024 + hh * 512 + 512],
                           start=(kc == 0), stop=(kc == 31))
            t_z = em.mark("pe")
            em.wait("dve", t_z)
            em.op("dve", "tensor_tensor", zg[nc4], psZ[p][:], G_sb[nc4][:], op=ALU.mult)
            zg_mark[nc4] = em.mark("dve")

        y_ev = [None] * 32
        st2_mark = [None] * 32
        em.waits("pe", zg_mark + [t_B])
        for ofc in range(32):
            p = ofc % 2
            if ofc >= 2:
                em.wait("pe", y_ev[ofc - 2])
            for nc4 in range(4):
                for hh in range(2):
                    em.opw("pe", [t_B], "matmul",
                           psZ[p][:, hh * 512:(hh + 1) * 512],
                           B_sb[:, nc4 * 4096 + ofc * 128: nc4 * 4096 + ofc * 128 + 128],
                           zg[nc4][:, hh * 512:(hh + 1) * 512],
                           start=(nc4 == 0), stop=(nc4 == 3))
            t_y = em.mark("pe")
            em.wait("dve", t_y)
            em.op("dve", "scalar_tensor_tensor", Xw[:, ofc * 1024:(ofc + 1) * 1024],
                  psZ[p][:], FFN_SCALING, Xw[:, ofc * 1024:(ofc + 1) * 1024],
                  op0=ALU.mult, op1=ALU.add)
            y_ev[ofc] = em.mark("dve")
            em.wait("act", y_ev[ofc])
            if ofc >= 1:
                em.wait("act", st2_mark[ofc - 1])
            em.op("act", "square", sqE, Xw[:, ofc * 1024:(ofc + 1) * 1024])
            t_sq = em.mark("act")
            em.waits("pe", [y_ev[ofc], t_sq])
            for hh in range(2):
                em.op("pe", "matmul", psS1b[:, hh * 512:(hh + 1) * 512], onec16[:],
                      Xw[:, ofc * 1024 + hh * 512: ofc * 1024 + hh * 512 + 512],
                      start=(ofc == 0), stop=(ofc == 31))
                em.op("pe", "matmul", psS2b[:, hh * 512:(hh + 1) * 512], onec16[:],
                      sqE[:, hh * 512:(hh + 1) * 512],
                      start=(ofc == 0), stop=(ofc == 31))
            st2_mark[ofc] = em.mark("pe")

        if debug:
            em.waits("sp", [y_ev[31]] + zg_mark)
            for i4 in range(4):
                em.dma("sp", "gdbg", d["dbg_zg"][:, i4 * 1024:(i4 + 1) * 1024], zg[i4])
            em.dma("sp", "gdbg", d["dbg_x2p"], Xw)
        em.wait("dve", st2_mark[31])
        em.op("dve", "tensor_scalar", rowE, psS1b[:], 1.0 / DM, None, op0=ALU.mult)
        em.op("dve", "tensor_scalar", rowE2, psS2b[:], 1.0 / DM, None, op0=ALU.mult)
        em.self_edge("dve")
        em.op("dve", "tensor_tensor", rowE3, rowE, rowE, op=ALU.mult)
        em.self_edge("dve")
        em.op("dve", "tensor_tensor", rowE2, rowE2, rowE3, op=ALU.subtract)
        em.self_edge("dve")
        em.op("dve", "tensor_scalar", rowE2, rowE2, EPS_LN, None, op0=ALU.add)
        t_var = em.mark("dve")
        em.wait("act", t_var)
        em.op("act", "activation", rowE2, rowE2, AF.Ln)
        em.self_edge("act")
        em.op("act", "activation", rowE2, rowE2, AF.Exp, scale=NEGH[0:1, :])
        t_rstd = em.mark("act")
        em.waits("pe", [t_rstd, y_ev[31], y_ev[30]])
        for hh in range(2):
            em.op("pe", "matmul", psZ[0][:, hh * 512:(hh + 1) * 512], oner32[:, 0:128],
                  rowE[0:1, hh * 512:(hh + 1) * 512], start=True, stop=True)
            em.op("pe", "matmul", psZ[1][:, hh * 512:(hh + 1) * 512], oner32[:, 0:128],
                  rowE2[0:1, hh * 512:(hh + 1) * 512], start=True, stop=True)
        t_bc = em.mark("pe")
        em.wait("dve", t_bc)
        em.op("dve", "tensor_copy", meanB2, psZ[0][:])
        em.op("dve", "tensor_copy", rstdB2, psZ[1][:])
        em.self_edge("dve")

        t_out = [None] * 32
        o_mark = [None] * 32
        for ofc in range(32):
            p = ofc % 2
            if ofc >= 2:
                em.waits("dve", [t_out[ofc - 2]])
            em.op("dve", "tensor_tensor", c32[p][:],
                  Xw[:, ofc * 1024:(ofc + 1) * 1024], meanB2, op=ALU.subtract)
            em.self_edge("dve")
            em.op("dve", "tensor_tensor", out_sb[p][:], c32[p][:], rstdB2, op=ALU.mult)
            o_mark[ofc] = em.mark("dve")
            q = "sp" if p == 0 else "pool"
            em.wait(q, o_mark[ofc])
            t_out[ofc] = em.dma(q, f"gout{p}",
                                d["out"][ofc * 128:(ofc + 1) * 128, :], out_sb[p][:])

        for p in range(2):
            em.op("sp", "wait_ge", em.sem_dma[f"gout{p}"], 16 * em.ndma[f"gout{p}"])
        if "gdbg" in em.sem_dma:
            em.op("sp", "wait_ge", em.sem_dma["gdbg"], 16 * em.ndma["gdbg"])


def assemble(res, inp):
    out = np.empty((2, S, DM), dtype=np.float32)
    for core in range(8):
        b, jj = core // 4, core % 4
        tiles = _slot_tiles(jj)
        cols = np.concatenate([np.arange(t * 128, (t + 1) * 128) for t in tiles])
        out[b, cols, :] = np.asarray(res.results[core]["out"]).T
    return out


def kernel(**inputs):
    in_maps = prepare_inputs(inputs)
    nc = build()
    res = run_bass_kernel_spmd(nc, in_maps, core_ids=list(range(8)))
    return assemble(res, inputs)


# version 07 (final)
# speedup vs baseline: 1.0431x; 1.0431x over previous
import math
import numpy as np
import ml_dtypes
from contextlib import ExitStack

import concourse.bass as bass
import concourse.mybir as mybir
from concourse.bass_utils import run_bass_kernel_spmd

F32 = mybir.dt.float32
BF16 = mybir.dt.bfloat16
AF = mybir.ActivationFunctionType
ALU = mybir.AluOpType
AX = mybir.AxisListType.X
BF = ml_dtypes.bfloat16

S = 4096
DM = 4096
H = 32
NOPE = 64
ROPE = 64
VD = 128
QHD = 128
KVR = 256
LATW = 320
NSLOT = 8
TRIPS = [1, 2, 3, 4, 1, 2, 3, 4]
EPS_RMS = 1e-6
EPS_LN = 1e-5
M_YARN = 0.1 * 1.0 * math.log(128.0) + 1.0
SOFTMAX_SCALE = (QHD ** -0.5) * M_YARN * M_YARN
FFN_SCALING = 2.0
VSTRIDE = 132

ENGS = ("sp", "act", "dve", "pool", "pe")
KB = 1024


def _yarn_cos_sin(seq_len, dim=ROPE, base=10000.0, factor=128.0, beta_fast=32.0,
                  beta_slow=1.0, orig_max=8192, mscale=1.0, mscale_all_dim=1.0):
    def corr_dim(r):
        return dim * math.log(orig_max / (r * 2 * math.pi)) / (2 * math.log(base))
    low = max(math.floor(corr_dim(beta_fast)), 0)
    high = min(math.ceil(corr_dim(beta_slow)), dim - 1)
    hi = high + 0.001 if low == high else float(high)
    ramp = np.clip((np.arange(dim // 2, dtype=np.float32) - low) / (hi - low), 0.0, 1.0)
    inv_freq_mask = 1.0 - ramp
    freq_extra = 1.0 / base ** (np.arange(0, dim, 2, dtype=np.float32) / dim)
    freq_inter = freq_extra / factor
    inv_freq = freq_inter * (1.0 - inv_freq_mask) + freq_extra * inv_freq_mask
    t = np.arange(seq_len, dtype=np.float32)
    freqs = np.outer(t, inv_freq)
    emb = np.concatenate([freqs, freqs], axis=-1)

    def gm(s, m):
        return 1.0 if s <= 1 else 0.1 * m * math.log(s) + 1.0
    ms = gm(factor, mscale) / gm(factor, mscale_all_dim)
    return (np.cos(emb) * ms).astype(np.float32), (np.sin(emb) * ms).astype(np.float32)


def _slot_tiles(j):
    return [8 * i + j for i in range(4)] + [8 * i + (7 - j) for i in range(4)]


def _deint_perm(n=64):
    p = np.empty(n, dtype=np.int64)
    p[:32] = np.arange(32) * 2
    p[32:] = np.arange(32) * 2 + 1
    return p


def _rot_mat():
    R = np.zeros((64, 64), dtype=np.float32)
    for i in range(32):
        R[32 + i, i] = -1.0
        R[i, 32 + i] = 1.0
    return R.astype(BF)


def prepare_inputs(inp):
    x = np.asarray(inp["L0_attn_x"])
    wq_a = np.asarray(inp["L0_attn_wq_a"])
    q_lnw = np.asarray(inp["L0_attn_q_a_ln_w"])
    wq_b = np.asarray(inp["L0_attn_wq_b"])
    wkv_a = np.asarray(inp["L0_attn_wkv_a"])
    kv_lnw = np.asarray(inp["L0_attn_kv_a_ln_w"])
    wkv_b = np.asarray(inp["L0_attn_wkv_b"])
    wo = np.asarray(inp["L0_attn_wo"])
    A_w = np.asarray(inp["L0_ffn_A_w"])
    B_w = np.asarray(inp["L0_ffn_B_w"])
    Wq_w = np.asarray(inp["L0_ffn_Wq_w"])
    keys = np.asarray(inp["L0_ffn_keys"])

    C = np.ascontiguousarray
    perm = _deint_perm()

    wkv_a_p = wkv_a.copy()
    wkv_a_p[KVR:LATW] = wkv_a_p[KVR:LATW][perm]
    wkvaT = C(wkv_a_p.T)
    wqaT = C(wq_a.T)
    wq_b_p = wq_b.reshape(H, QHD, 1024).copy()
    wq_b_p[:, NOPE:] = wq_b_p[:, NOPE + perm]
    wqbT = C(wq_b_p.reshape(H * QHD, 1024).T)
    wkvbT = C(wkv_b.T)
    wo_sta = C(np.transpose(wo.reshape(32, 128, 32, 128), (0, 3, 2, 1)))
    A_sta = C(A_w.T.reshape(32, 128, 512))
    B_sta = C(B_w.T.reshape(4, 128, 4096))
    WqT = C(Wq_w.T.reshape(32, 128, 16))
    keysT = C(keys.T)
    lnw_q = C(q_lnw.reshape(8, 128).T)
    lnw_kv = C(kv_lnw.reshape(2, 128).T)

    cos, sin = _yarn_cos_sin(S)
    cosF = C(cos.T)
    sinF = C(sin.T)

    ident16 = np.eye(128, dtype=BF)
    ident32 = np.eye(128, dtype=np.float32)
    rot = _rot_mat()
    ones_col32 = np.ones((128, 1), dtype=np.float32)
    ones_col16 = np.ones((128, 1), dtype=BF)
    ones_row16 = np.ones((1, 512), dtype=BF)
    ones_row32 = np.ones((1, 128), dtype=np.float32)
    cvec = np.zeros((128, 8), dtype=np.float32)
    cvec[:, 0] = SOFTMAX_SCALE
    cvec[:, 1] = -0.5
    R_exp = np.zeros((64, 512), dtype=np.float32)
    for c in range(4):
        for m in range(128):
            R_exp[16 * c + m // 8, 128 * c + m] = 1.0

    shared = dict(
        wkvaT=wkvaT, wqaT=wqaT, wqbT=wqbT, wkvbT=wkvbT, wo_sta=wo_sta,
        A_sta=A_sta, B_sta=B_sta, WqT=WqT, keysT=keysT,
        lnw_q=lnw_q, lnw_kv=lnw_kv, cosF=cosF, sinF=sinF,
        ident16=ident16, ident32=ident32, rot=rot,
        ones_col32=ones_col32, ones_col16=ones_col16,
        ones_row16=ones_row16, ones_row32=ones_row32, cvec=cvec, R_exp=R_exp,
    )

    in_maps = []
    for core in range(8):
        b, jj = core // 4, core % 4
        tiles = _slot_tiles(jj)
        cols = np.concatenate([np.arange(t * 128, (t + 1) * 128) for t in tiles])
        xT = C(x[b].T)
        xoT = C(xT[:, cols])
        cos_o = C(cos[cols].T)
        sin_o = C(sin[cols].T)
        maskPT = np.zeros((2, 128, 1024), dtype=BF)
        for fam, toff in enumerate((jj, 7 - jj)):
            for sblk in range(8):
                for k in range(128):
                    lo = (sblk * 128 + k) - toff * 128
                    if lo <= 0:
                        maskPT[fam, k, sblk * 128:(sblk + 1) * 128] = BF(1.0)
                    elif lo < 128:
                        maskPT[fam, k, sblk * 128 + lo:(sblk + 1) * 128] = BF(1.0)
        m = dict(shared)
        m.update(xT=xT, xoT=xoT, cos_o=cos_o, sin_o=sin_o,
                 maskPT=C(maskPT.reshape(2 * 128, 1024)))
        in_maps.append(m)
    return in_maps


_LAST_EM = [None]


class Em:
    def __init__(self, nc, ctx):
        self.nc = nc
        self.ctx = ctx
        self.stream = {e: [] for e in ENGS}
        self.ms = {e: 0 for e in ENGS}
        self.ndma = {}
        self.sem_ms = {e: ctx.enter_context(nc.semaphore(f"ms_{e}")) for e in ENGS}
        self.sem_dma = {}
        self.waited = {}

    TRACE_LINES = False

    def op(self, e, meth, *args, **kw):
        self.stream[e].append((meth, args, kw, None, 0, (), self._ln()))

    def _ln(self):
        if not Em.TRACE_LINES:
            return 0
        import sys as _s
        f = _s._getframe(2)
        return f.f_lineno

    def opw(self, e, toks, meth, *args, **kw):
        att = []
        for t in toks:
            if t is None:
                continue
            kind, p, v = t
            sem = self.sem_ms[p] if kind == "ms" else self.sem_dma[p]
            val = v if kind == "ms" else 16 * v
            att.append((sem, val))
        self.stream[e].append((meth, args, kw, None, 0, att, self._ln()))

    def mark(self, e):
        self.ms[e] += 1
        self.stream[e].append(("drain", (), {}, self.sem_ms[e], 1, (), self._ln()))
        return ("ms", e, self.ms[e])

    def self_edge(self, e):
        t = self.mark(e)
        self.op(e, "wait_ge", self.sem_ms[e], t[2])

    def dma(self, e, grp, out_ap, in_ap, **kw):
        if grp not in self.sem_dma:
            self.sem_dma[grp] = self.ctx.enter_context(self.nc.semaphore(f"dq_{grp}"))
            self.ndma[grp] = 0
        self.ndma[grp] += 1
        self.stream[e].append(("dma_start", (out_ap, in_ap), kw, self.sem_dma[grp], 16, (), self._ln()))
        return ("dma", grp, self.ndma[grp])

    def wait(self, e, tok):
        if tok is None:
            return
        kind, p, v = tok
        if kind == "ms" and p == e:
            return
        key = (e, kind, p)
        if self.waited.get(key, 0) >= v:
            return
        self.waited[key] = v
        sem = self.sem_ms[p] if kind == "ms" else self.sem_dma[p]
        val = v if kind == "ms" else 16 * v
        self.op(e, "wait_ge", sem, val)

    def waits(self, e, toks):
        best = {}
        for t in toks:
            if t is None:
                continue
            k = (t[0], t[1])
            if k not in best or best[k][2] < t[2]:
                best[k] = t
        for t in best.values():
            self.wait(e, t)

    def barrier(self):
        toks = [self.mark(e) for e in ENGS]
        for e in ENGS:
            self.waits(e, toks)
        return dict(zip(ENGS, toks))

    def preamble(self, tok):
        for e in ENGS:
            self.wait(e, tok)
            for _ in range(10):
                self.op(e, "nop")


def _replay(nc, em, ctx):
    block = ctx.enter_context(nc.Block())
    sections = {"sp": block.sync, "act": block.scalar, "dve": block.vector,
                "pool": block.gpsimd, "pe": block.tensor}

    def runner(e):
        def body(h):
            for entry in em.stream[e]:
                meth, args, kw, sem, n = entry[:5]
                ins = getattr(h, meth)(*args, **kw)
                for s, v in (entry[5] if len(entry) > 5 else ()):
                    ins._wait_ge(s, v)
                if sem is not None:
                    ins.then_inc(sem, n)
        return body
    for e in ENGS:
        sections[e](runner(e))


AR_BYTES = 204 * KB


class Arena:

    def __init__(self, nc):
        self.t = nc.alloc_sbuf_tensor("AR", [128, AR_BYTES // 2], BF16)

    def v(self, off_b, nbytes, dt=BF16, parts=(0, 128)):
        assert off_b % 4 == 0 and off_b + nbytes <= AR_BYTES, (off_b, nbytes)
        ap = self.t[:, off_b // 2:(off_b + nbytes) // 2]
        if dt != BF16:
            ap = ap.bitcast(dt)
        if parts != (0, 128):
            ap = ap[parts[0]:parts[1], :]
        return ap


def build(debug=False, race_detect=True, upto="F"):
    nc = bass.Bass("TRN2", target_bir_lowering=False,
                   detect_race_conditions=race_detect)
    d = {}

    def din(name, shape, dt):
        d[name] = nc.dram_tensor(name, list(shape), dt, kind="ExternalInput").ap()

    def dout(name, shape, dt):
        d[name] = nc.dram_tensor(name, list(shape), dt, kind="ExternalOutput").ap()

    din("xT", (4096, 4096), F32)
    din("xoT", (4096, 1024), F32)
    din("wkvaT", (4096, 320), F32)
    din("wqaT", (4096, 1024), F32)
    din("wqbT", (1024, 4096), F32)
    din("wkvbT", (256, 6144), F32)
    din("wo_sta", (32, 128, 32, 128), F32)
    din("A_sta", (32, 128, 512), F32)
    din("B_sta", (4, 128, 4096), F32)
    din("WqT", (32, 128, 16), F32)
    din("keysT", (16, 64), F32)
    din("lnw_q", (128, 8), F32)
    din("lnw_kv", (128, 2), F32)
    din("cosF", (64, 4096), F32)
    din("sinF", (64, 4096), F32)
    din("cos_o", (64, 1024), F32)
    din("sin_o", (64, 1024), F32)
    din("maskPT", (256, 1024), BF16)
    din("ident16", (128, 128), BF16)
    din("ident32", (128, 128), F32)
    din("rot", (64, 64), BF16)
    din("ones_col32", (128, 1), F32)
    din("ones_col16", (128, 1), BF16)
    din("ones_row16", (1, 512), BF16)
    din("ones_row32", (1, 128), F32)
    din("cvec", (128, 8), F32)
    din("R_exp", (64, 512), F32)
    dout("out", (4096, 1024), F32)
    if debug:
        dout("dbg_cnorm", (256, 4096), BF16)
        dout("dbg_kpe", (64, 4096), BF16)
        dout("dbg_qa", (128, 8192), BF16)
        dout("dbg_q", (128, H * 1024), BF16)
        dout("dbg_attn", (128, H * 1024), BF16)
        dout("dbg_x1", (128, 32 * 1024), BF16)
        dout("dbg_gate", (128, 512), F32)
        dout("dbg_stm", (128, 512), F32)
        dout("dbg_zg", (128, 4096), BF16)
        dout("dbg_x2p", (128, 32 * 1024), BF16)
        dout("dbg_kf", (256, 4096), BF16)
        dout("dbg_vx", (256, 32 * VSTRIDE), BF16)
        dout("dbg_acc", (256, NSLOT * VSTRIDE), F32)
        dout("dbg_P", (256, 1024), BF16)
        dout("dbg_mr", (128, NSLOT), F32)
        dout("dbg_it", (128, 8192), F32)

    with ExitStack() as ctx:
        em = Em(nc, ctx)
        _LAST_EM[0] = em
        _emit(nc, em, d, ctx, debug, upto)
        _replay(nc, em, ctx)
    return nc


def _emit(nc, em, d, ctx, debug, upto="F"):
    def _finish():
        for grp, n in em.ndma.items():
            em.op("sp", "wait_ge", em.sem_dma[grp], 16 * n)

    ar = Arena(nc)

    def sbt(name, shape, dt):
        return ctx.enter_context(nc.sbuf_tensor(name, list(shape), dt))

    consts = sbt("consts", (128, 8), F32)
    SCALE = consts[:, 0:1]
    NEGH = consts[:, 1:2]
    idn16 = sbt("idn16", (128, 128), BF16)
    idn32 = sbt("idn32", (128, 128), F32)
    rot_sb = sbt("rot_sb", (128, 64), BF16)
    onec32 = sbt("onec32", (128, 1), F32)
    onec16 = sbt("onec16", (128, 1), BF16)
    oner16 = sbt("oner16", (1, 512), BF16)
    oner32 = sbt("oner32", (1, 128), F32)
    lnwq_sb = sbt("lnwq_sb", (128, 8), F32)
    lnwkv_sb = sbt("lnwkv_sb", (128, 2), F32)

    mask2 = ar.v(0, 4 * KB)
    cos_o_sb = ar.v(4 * KB, 2 * KB, BF16, (64, 128))
    sin_o_sb = ar.v(6 * KB, 2 * KB, BF16, (64, 128))
    woslab = [ar.v(8 * KB + i * 8 * KB, 8 * KB) for i in range(2)]
    cnorm = [ar.v(24 * KB + c * 8 * KB, 8 * KB) for c in range(2)]
    kpe = ar.v(40 * KB, 8 * KB, BF16, (0, 64))
    Q = ar.v(48 * KB, 64 * KB)

    tcs = None
    for name, apx in [("cvec", consts[:]), ("ident16", idn16[:]), ("ident32", idn32[:]),
                      ("ones_col32", onec32[:]), ("ones_col16", onec16[:]),
                      ("ones_row16", oner16[:]), ("ones_row32", oner32[:]),
                      ("lnw_q", lnwq_sb[:]), ("lnw_kv", lnwkv_sb[:])]:
        tcs = em.dma("sp", "gconst", apx, d[name])
    em.dma("sp", "gconst", rot_sb[0:64, :], d["rot"])
    em.dma("sp", "gconst", rot_sb[64:128, :], d["rot"])
    em.dma("sp", "gconst", mask2[:, 0:1024], d["maskPT"][0:128, :])
    tcs = em.dma("sp", "gconst", mask2[:, 1024:2048], d["maskPT"][128:256, :])
    em.dma("pool", "gcoso", cos_o_sb, d["cos_o"])
    t_coso = em.dma("pool", "gcoso", sin_o_sb, d["sin_o"])
    em.preamble(tcs)

    xfm = [ar.v(112 * KB + i * 32 * KB, 32 * KB) for i in range(2)]
    wkva_sb = ar.v(176 * KB, 20 * KB)
    cosF_sb = ar.v(196 * KB, 8 * KB, BF16, (0, 64))
    sinF_sb = ar.v(60 * KB, 8 * KB, BF16, (0, 64))
    pescrA = ar.v(48 * KB, 1 * KB, BF16, (0, 64))
    t1A = ar.v(50 * KB, 2 * KB, F32, (0, 64))
    t2A = ar.v(52 * KB, 1 * KB, BF16, (0, 64))
    rstdA = ar.v(54 * KB, 2 * KB, F32)
    sqA = ar.v(56 * KB, 2 * KB, F32)
    rowA = ar.v(58 * KB, 2 * KB, F32, (0, 1))

    with ExitStack() as pctx:
        psl = lambda n, s, t: pctx.enter_context(nc.psum_tensor(n, list(s), t))
        psL = [psl(f"psL{i}", (128, 512), F32) for i in range(3)]
        psS = psl("psS", (1, 512), F32)
        psB = psl("psB", (128, 512), F32)
        psR = psl("psR", (64, 512), F32)

        em.dma("pool", "gcosf", cosF_sb, d["cosF"])
        t_cosF = em.dma("pool", "gcosf", sinF_sb, d["sinF"])
        wkva_ap = d["wkvaT"].rearrange("(kc p) w -> kc p w", p=128).transpose([1, 0, 2])
        wkva_dst = wkva_sb.rearrange("p (kc w) -> p kc w", w=320)
        t_wkva = None
        for i in range(4):
            t_wkva = em.dma("pool", "gwkva", wkva_dst[:, 8 * i:8 * i + 8, :],
                            wkva_ap[:, 8 * i:8 * i + 8, :])
        def load_xfm(g):
            gp = g % 2
            tk = None
            for fc in range(32):
                tk = em.dma("pool", f"gxfm{gp}", xfm[gp][:, fc * 512:(fc + 1) * 512],
                            d["xT"][fc * 128:(fc + 1) * 128, g * 512:(g + 1) * 512])
            return tk

        x_tok = [None] * 8
        x_tok[0] = load_xfm(0)
        x_tok[1] = load_xfm(1)

        mm_mark = [None] * 8
        sq_done = [None] * 8
        dve_done = [None] * 8
        for g in range(8):
            gp = g % 2
            gc = slice(g * 512, (g + 1) * 512)
            em.waits("pe", [x_tok[g], t_wkva])
            if g >= 1:
                em.waits("pe", [dve_done[g - 1], sq_done[g - 1]])
            for ofc in range(3):
                mw = 128 if ofc < 2 else 64
                for kc in range(32):
                    em.opw("pe", [t_wkva], "matmul", psL[ofc][0:mw, :],
                           wkva_sb[:, kc * 320 + ofc * 128: kc * 320 + ofc * 128 + mw],
                           xfm[gp][:, kc * 512:(kc + 1) * 512],
                           start=(kc == 0), stop=(kc == 31))
            t_mm = em.mark("pe")
            mm_mark[g] = t_mm
            if g + 2 < 8:
                em.wait("pool", t_mm)
                x_tok[g + 2] = load_xfm(g + 2)

            em.wait("act", t_mm)
            em.op("act", "square", sqA, psL[0][:])
            t_sq0 = em.mark("act")
            em.wait("pe", t_sq0)
            em.op("pe", "matmul", psS[:], onec32[:], sqA, start=True, stop=False)
            t_s0 = em.mark("pe")
            em.wait("act", t_s0)
            em.op("act", "square", sqA, psL[1][:])
            em.op("act", "copy", pescrA, psL[2][0:64, :])
            t_sq1 = em.mark("act")
            sq_done[g] = t_sq1
            em.wait("pe", t_sq1)
            em.op("pe", "matmul", psS[:], onec32[:], sqA, start=False, stop=True)
            t_ssq = em.mark("pe")
            em.wait("dve", t_ssq)
            em.op("dve", "tensor_scalar", rowA, psS[:], 1.0 / KVR, EPS_RMS,
                  op0=ALU.mult, op1=ALU.add)
            t_var = em.mark("dve")
            em.wait("act", t_var)
            em.op("act", "activation", rowA, rowA, AF.Ln)
            em.self_edge("act")
            em.op("act", "activation", rowA, rowA, AF.Exp, scale=NEGH[0:1, :])
            t_rstd = em.mark("act")
            em.wait("pe", t_rstd)
            em.op("pe", "matmul", psB[:], oner32[:, 0:128], rowA, start=True, stop=True)
            t_bc = em.mark("pe")
            em.wait("dve", t_bc)
            em.op("dve", "tensor_copy", rstdA, psB[:])
            em.self_edge("dve")
            for c in range(2):
                em.op("dve", "scalar_tensor_tensor", cnorm[c][:, gc], psL[c][:],
                      lnwkv_sb[:, c:c + 1], rstdA, op0=ALU.mult, op1=ALU.mult)
            em.waits("dve", [t_cosF, sq_done[g]])
            em.op("dve", "tensor_tensor", t1A, pescrA, cosF_sb[:, gc], op=ALU.mult)
            em.op("dve", "tensor_tensor", t2A, pescrA, sinF_sb[:, gc], op=ALU.mult)
            t_rope = em.mark("dve")
            em.wait("pe", t_rope)
            em.op("pe", "matmul", psR[:], rot_sb[0:64, :], t2A, start=True, stop=True)
            t_rot = em.mark("pe")
            em.wait("dve", t_rot)
            em.op("dve", "tensor_tensor", kpe[:, gc], t1A, psR[:], op=ALU.add)
            dve_done[g] = em.mark("dve")
        em.barrier()
        if debug:
            for c in range(2):
                em.dma("sp", "gdbg", d["dbg_cnorm"][c * 128:(c + 1) * 128, :], cnorm[c][:])
            em.dma("sp", "gdbg", d["dbg_kpe"], kpe)
        if upto == "A":
            _finish()
            return

    xo16 = ar.v(48 * KB, 64 * KB)
    wqa_sb = ar.v(112 * KB, 64 * KB)
    qa_sb = ar.v(176 * KB, 16 * KB)
    sqB = ar.v(192 * KB, 4 * KB, F32)
    rowB = ar.v(196 * KB, 4 * KB, F32, (0, 1))
    rstdB = ar.v(200 * KB, 2 * KB)

    with ExitStack() as pctx:
        psl = lambda n, s, t: pctx.enter_context(nc.psum_tensor(n, list(s), t))
        psB2 = [psl(f"psB2{i}", (128, 1024), F32) for i in range(2)]
        psSq = psl("psSq", (1, 1024), F32)
        psBc = psl("psBc", (128, 1024), F32)

        t_xo = None
        for fc in range(32):
            t_xo = em.dma("pool", "gxo", xo16[:, fc * 1024:(fc + 1) * 1024],
                          d["xoT"][fc * 128:(fc + 1) * 128, :])
        wqa_ap = d["wqaT"].rearrange("(kc p) m -> kc p m", p=128).transpose([1, 0, 2])
        wqa_dst = wqa_sb.rearrange("p (kc m) -> p kc m", m=1024)
        t_wqa = None
        for i in range(4):
            t_wqa = em.dma("pool", "gwqa", wqa_dst[:, 8 * i:8 * i + 8, :],
                           wqa_ap[:, 8 * i:8 * i + 8, :])

        ev_mark = [None] * 8
        sqm = [None] * 8
        for ofc in range(8):
            p = ofc % 2
            em.waits("pe", [t_xo, t_wqa])
            if ofc >= 2:
                em.waits("pe", [ev_mark[ofc - 2], sqm[ofc - 2]])
            for kc in range(32):
                for hh in range(2):
                    em.opw("pe", [t_wqa], "matmul",
                           psB2[p][:, hh * 512:(hh + 1) * 512],
                           wqa_sb[:, kc * 1024 + ofc * 128: kc * 1024 + ofc * 128 + 128],
                           xo16[:, kc * 1024 + hh * 512: kc * 1024 + hh * 512 + 512],
                           start=(kc == 0), stop=(kc == 31))
            t_mm = em.mark("pe")
            if upto != "B0":
                em.wait("act", t_mm)
                actf = "copy" if upto == "B0c" else "square"
                em.op("act", actf, sqB[:, 0:512], psB2[p][:, 0:512])
                em.op("act", actf, sqB[:, 512:1024], psB2[p][:, 512:1024])
                sqm[ofc] = em.mark("act")
                if upto not in ("B0a", "B0c"):
                    em.wait("pe", sqm[ofc])
                    for hh in range(2):
                        em.op("pe", "matmul", psSq[:, hh * 512:(hh + 1) * 512], onec32[:],
                              sqB[:, hh * 512:(hh + 1) * 512],
                              start=(ofc == 0), stop=(ofc == 7))
            else:
                sqm[ofc] = t_mm
            em.waits("dve", [t_mm, sqm[ofc]])
            em.op("dve", "tensor_scalar", qa_sb[:, ofc * 1024:(ofc + 1) * 1024],
                  psB2[p][:], lnwq_sb[:, ofc:ofc + 1], None, op0=ALU.mult)
            ev_mark[ofc] = em.mark("dve")
        t_sqq = em.mark("pe")
        if upto in ("B0", "B0a", "B0c"):
            if debug:
                em.dma("sp", "gdbg", d["dbg_qa"], qa_sb)
            _finish()
            return
        if upto == "B1":
            if debug:
                em.dma("sp", "gdbg", d["dbg_qa"], qa_sb)
            _finish()
            return
        em.wait("dve", t_sqq)
        em.op("dve", "tensor_scalar", rowB, psSq[:], 1.0 / 1024.0, EPS_RMS,
              op0=ALU.mult, op1=ALU.add)
        t_var = em.mark("dve")
        em.wait("act", t_var)
        em.op("act", "activation", rowB, rowB, AF.Ln)
        em.self_edge("act")
        em.op("act", "activation", rowB, rowB, AF.Exp, scale=NEGH[0:1, :])
        t_rstd = em.mark("act")
        em.wait("pe", t_rstd)
        for hh in range(2):
            em.op("pe", "matmul", psBc[:, hh * 512:(hh + 1) * 512], oner32[:, 0:128],
                  rowB[0:1, hh * 512:(hh + 1) * 512], start=True, stop=True)
        t_bc = em.mark("pe")
        em.wait("dve", t_bc)
        em.op("dve", "tensor_copy", rstdB, psBc[:])
        em.self_edge("dve")
        for ofc in range(8):
            em.op("dve", "tensor_tensor", qa_sb[:, ofc * 1024:(ofc + 1) * 1024],
                  qa_sb[:, ofc * 1024:(ofc + 1) * 1024], rstdB, op=ALU.mult)
        em.mark("dve")
        em.barrier()
        if debug:
            em.dma("sp", "gdbg", d["dbg_qa"], qa_sb)
        if upto == "B":
            _finish()
            return

    wqb_sb = ar.v(112 * KB, 64 * KB)
    pe_scr = ar.v(192 * KB, 2 * KB, BF16, (64, 128))
    a_scr = ar.v(194 * KB, 4 * KB, F32, (64, 128))
    b_scr = ar.v(198 * KB, 2 * KB, BF16, (64, 128))

    with ExitStack() as pctx:
        psl = lambda n, s, t: pctx.enter_context(nc.psum_tensor(n, list(s), t))
        psC = [psl(f"psC{i}", (128, 1024), F32) for i in range(2)]
        psRo = psl("psRo", (128, 1024), F32)

        wqb_ap = d["wqbT"].rearrange("(kc p) m -> kc p m", p=128).transpose([1, 0, 2])
        wqb_dst = wqb_sb.rearrange("p (kc m) -> p kc m", m=4096)
        t_wqb = None
        for i in range(8):
            t_wqb = em.dma("pool", "gwqb", wqb_dst[:, i:i + 1, :], wqb_ap[:, i:i + 1, :])

        cp_mark = [None] * H
        rope_mark = [None] * H
        q_mark = [None] * H
        for h in range(H):
            p = h % 2
            hc = slice(h * 1024, (h + 1) * 1024)
            em.waits("pe", [t_wqb])
            if h >= 2:
                em.wait("pe", cp_mark[h - 2])
            for kc in range(8):
                for hh in range(2):
                    em.opw("pe", [t_wqb], "matmul", psC[p][:, hh * 512:(hh + 1) * 512],
                           wqb_sb[:, kc * 4096 + h * 128: kc * 4096 + h * 128 + 128],
                           qa_sb[:, kc * 1024 + hh * 512: kc * 1024 + hh * 512 + 512],
                           start=(kc == 0), stop=(kc == 7))
            t_mm = em.mark("pe")
            em.wait("act", t_mm)
            if h >= 1:
                em.wait("act", rope_mark[h - 1])
            em.op("act", "copy", Q[0:64, hc], psC[p][0:64, :])
            em.op("act", "copy", pe_scr, psC[p][64:128, :])
            cp_mark[h] = em.mark("act")
            em.waits("pool", [cp_mark[h], t_coso])
            if h >= 1:
                em.waits("pool", [q_mark[h - 1], rope_mark[h - 1]])
            em.op("pool", "tensor_tensor", a_scr, pe_scr, cos_o_sb, op=ALU.mult)
            em.op("pool", "tensor_tensor", b_scr, pe_scr, sin_o_sb, op=ALU.mult)
            rope_mark[h] = em.mark("pool")
            em.wait("pe", rope_mark[h])
            if h >= 1:
                em.wait("pe", q_mark[h - 1])
            for hh in range(2):
                em.op("pe", "matmul", psRo[64:128, hh * 512:(hh + 1) * 512],
                      rot_sb[64:128, :], b_scr[:, hh * 512:(hh + 1) * 512],
                      start=True, stop=True, tile_position=(64, 64))
            t_rot = em.mark("pe")
            em.wait("dve", t_rot)
            em.op("dve", "tensor_tensor", Q[64:128, hc], a_scr, psRo[64:128, :],
                  op=ALU.add)
            q_mark[h] = em.mark("dve")
        em.barrier()
        if debug:
            em.dma("sp", "gdbg", d["dbg_q"], Q)
        if upto == "C":
            _finish()
            return

    wkvb_sb = [ar.v(112 * KB + c * 12 * KB, 12 * KB) for c in range(2)]
    Kf = [ar.v(136 * KB + i * 8 * KB, 8 * KB) for i in range(2)]
    Vx = [ar.v(152 * KB + i * 8448, 8448) for i in range(2)]
    P_sb = [ar.v(169 * KB + i * 2 * KB, 2 * KB) for i in range(2)]
    PT_sb = [ar.v(173 * KB + i * 2 * KB, 2 * KB) for i in range(2)]
    accT = [ar.v(177 * KB + i * 4224, 4224, F32) for i in range(2)]
    m_run = ar.v(186 * KB, 32, F32)
    mnb = [ar.v(186 * KB + 64 + 32 * i, 16) for i in range(2)]
    mx_c = ar.v(186 * KB + 128, 4, F32)
    f_c = [ar.v(186 * KB + 136 + 4 * i, 4, F32) for i in range(2)]
    diff8 = ar.v(186 * KB + 160, 32, F32)
    mrow = [ar.v(186 * KB + 256 + 256 * i, 256, BF16, (0, 1)) for i in range(2)]
    att_tm = ar.v(187 * KB, 256)
    rcp_c = ar.v(187 * KB + 256, 4, F32)

    dbg_it_sb = ar.v(188 * KB, 16 * KB, F32)
    t_wo01 = None
    wo_src = d["wo_sta"].rearrange("o p hc c -> o p (hc c)")
    with ExitStack() as pctx:
        psl = lambda n, s, t: pctx.enter_context(nc.psum_tensor(n, list(s), t))
        sA = psl("sA", (128, 1024), F32)
        sB = psl("sB", (128, 1024), F32)
        ptP = [psl(f"ptP{i}", (128, 1024), BF16) for i in range(2)]
        mix = psl("mix", (128, 512), F32)
        mix2 = psl("mix2", (128, 512), F32)
        pv_ = [mix[:, 0:129], mix2[:, 0:129]]
        mixb = mix[:].bitcast(BF16)
        atp = mixb[:, 640:768]
        scores = [sA, sB]

        t_wkvb = None
        for c in range(2):
            t_wkvb = em.dma("pool", "gwkvb", wkvb_sb[c][:], d["wkvbT"][c * 128:(c + 1) * 128, :])
        for i in range(2):
            em.op("dve", "memset",
                  Vx[i].rearrange("p (s w) -> p s w", w=VSTRIDE)[:, :, 128:129], 1.0)
            em.op("pool", "tensor_copy", Kf[i][64:128, :], kpe)
        t_init = em.mark("dve")
        t_init_p = em.mark("pool")

        t_wo0 = em.dma("pool", "gwo0", woslab[0][:], wo_src[0])
        t_wo1 = em.dma("pool", "gwo1", woslab[1][:], wo_src[1])
        t_wo01 = (t_wo0, t_wo1)

        iters = []
        first_it = {}
        for h in range(H):
            for s in range(NSLOT):
                for kb in range(TRIPS[s]):
                    if s == 0 and kb == 0:
                        first_it[h] = len(iters)
                    iters.append(dict(h=h, s=s, kb=kb, first=(kb == 0),
                                      last=(kb == TRIPS[s] - 1)))
        N = len(iters)

        tok_qk = [None] * (N + 2)
        tok_exp = [None] * N
        tok_pvt = [None] * (N + 1)
        tok_evac = [None] * N
        fin_done = [None, None]
        tok_flash_done = {}
        tok_exp_ready = {}
        fin_pe_prev = [None]
        fin_act_prev = [None]

        def emit_expansion(h, bank):
            hp = h % 2
            war = tok_flash_done.get(h - 2)
            ev = []
            kn_ev = [None] * 8
            v_ev = [None] * 8
            em.waits("pe", [t_wkvb])
            for b in range(8):
                if b >= 1:
                    em.wait("pe", kn_ev[b - 1])
                for c in range(2):
                    em.opw("pe", [t_wkvb], "matmul", bank[0:64, 0:512],
                           wkvb_sb[c][:, h * 192: h * 192 + 64],
                           cnorm[c][:, b * 512:(b + 1) * 512],
                           start=(c == 0), stop=(c == 1))
                t_kn = em.mark("pe")
                em.waits("dve", [t_kn, war, t_init, t_init_p])
                em.op("dve", "tensor_copy", Kf[hp][0:64, b * 512:(b + 1) * 512],
                      bank[0:64, 0:512])
                kn_ev[b] = em.mark("dve")
                ev.append(kn_ev[b])
                if b >= 1:
                    em.wait("pe", v_ev[b - 1])
                for sub in range(4):
                    kk = b * 4 + sub
                    for c in range(2):
                        em.opw("pe", [t_wkvb], "matmul",
                               bank[:, 512 + sub * 128: 512 + sub * 128 + 128],
                               cnorm[c][:, kk * 128:(kk + 1) * 128],
                               wkvb_sb[c][:, h * 192 + 64: h * 192 + 192],
                               start=(c == 0), stop=(c == 1))
                t_v = em.mark("pe")
                em.waits("act", [t_v, war, t_init, t_init_p])
                dst = Vx[hp][:, b * 4 * VSTRIDE:(b + 1) * 4 * VSTRIDE]
                dst = dst.rearrange("p (s w) -> p s w", w=VSTRIDE)[:, :, 0:128]
                em.op("act", "copy", dst,
                      bank[:, 512:1024].rearrange("p (s w) -> p s w", w=128))
                v_ev[b] = em.mark("act")
                ev.append(v_ev[b])
            tok_exp_ready[h] = ev

        def emit_qk(it):
            cu = iters[it]
            par = it % 2
            hp = cu["h"] % 2
            if it - 2 >= 0:
                em.wait("pe", tok_exp[it - 2])
            em.waits("pe", tok_exp_ready.get(cu["h"], []))
            kb = cu["kb"]
            lhsQ = Q[:, cu["h"] * 1024 + cu["s"] * 128: cu["h"] * 1024 + cu["s"] * 128 + 128]
            for hh in range(2):
                em.op("pe", "matmul", scores[par][:, hh * 512:(hh + 1) * 512],
                      lhsQ, Kf[hp][:, kb * 1024 + hh * 512: kb * 1024 + hh * 512 + 512],
                      start=True, stop=True)
            tok_qk[it] = em.mark("pe")

        emit_expansion(0, scores[0])
        emit_qk(0)

        for r in range(N + 1):
            prev = iters[r - 1] if r >= 1 else None
            cur = iters[r] if r < N else None
            nxt = iters[r + 1] if r + 1 < N else None

            if cur is not None:
                par = r % 2
                em.wait("act", tok_qk[r])
                if r - 2 >= 0:
                    em.wait("act", tok_pvt[r - 2])
                em.op("act", "activation", P_sb[par][:], scores[par][:], AF.Exp,
                      scale=SCALE)
                tok_exp[r] = em.mark("act")

            if nxt is not None:
                if cur is not None and nxt["h"] != cur["h"]:
                    if r - 1 >= 0:
                        em.wait("pe", tok_exp[r - 1])
                    emit_expansion(nxt["h"], scores[(r + 1) % 2])
                emit_qk(r + 1)

            if prev is not None:
                parp = (r - 1) % 2
                sp = prev["s"] % 2
                em.wait("pe", tok_evac[r - 1])
                if prev["first"] and fin_done[sp] is not None:
                    em.wait("pe", fin_done[sp])
                em.waits("pe", [fin_act_prev[0]])
                hpp = prev["h"] % 2
                kbp = prev["kb"]
                for t in range(8):
                    em.opw("pe", [tok_evac[r - 1]], "matmul", pv_[sp],
                           PT_sb[parp][:, t * 128:(t + 1) * 128],
                           Vx[hpp][:, (kbp * 8 + t) * VSTRIDE:
                                     (kbp * 8 + t) * VSTRIDE + 129],
                           start=(prev["first"] and t == 0),
                           stop=(prev["last"] and t == 7))

            if cur is not None:
                em.wait("pe", tok_exp[r])
                if r - 2 >= 0:
                    em.wait("pe", tok_evac[r - 2])
                for t in range(8):
                    em.opw("pe", [tok_exp[r]], "transpose",
                           ptP[par][:, t * 128:(t + 1) * 128],
                           P_sb[par][:, t * 128:(t + 1) * 128], idn16[:])
            tok_pvt[r] = em.mark("pe")

            if prev is not None and prev["last"]:
                sp = prev["s"] % 2
                h_, sf = prev["h"], prev["s"]
                em.wait("dve", tok_pvt[r])
                if fin_pe_prev[0] is not None:
                    em.wait("dve", fin_pe_prev[0])
                em.op("dve", "reciprocal", rcp_c, pv_[sp][:, 128:129])
                em.self_edge("dve")
                em.op("dve", "tensor_tensor", att_tm, pv_[sp][:, 0:128],
                      rcp_c.broadcast_to((128, 128)), op=ALU.mult)
                t_fin = em.mark("dve")
                fin_done[sp] = t_fin
                em.waits("pe", [t_fin, fin_act_prev[0]])
                em.opw("pe", [t_fin], "transpose", atp, att_tm, idn16[:])
                t_atp = em.mark("pe")
                fin_pe_prev[0] = t_atp
                em.wait("dve", t_atp)
                em.op("dve", "tensor_copy",
                      Q[:, h_ * 1024 + sf * 128: h_ * 1024 + sf * 128 + 128],
                      atp)
                fin_act_prev[0] = em.mark("dve")
                if sf == NSLOT - 1:
                    tok_flash_done[h_] = fin_act_prev[0]

            if cur is not None:
                e = "dve"
                em.wait(e, tok_pvt[r])
                if cur["last"]:
                    fam = 0 if cur["s"] < 4 else 1
                    em.op(e, "tensor_tensor", PT_sb[par][:], ptP[par][:],
                          mask2[:, fam * 1024:(fam + 1) * 1024], op=ALU.mult)
                elif e == "act":
                    em.op(e, "copy", PT_sb[par][:], ptP[par][:])
                else:
                    em.op(e, "tensor_copy", PT_sb[par][:], ptP[par][:])
                tok_evac[r] = em.mark(e)

        em.barrier()
        if debug:
            em.dma("sp", "gdbg", d["dbg_attn"], Q)
            for i in range(2):
                em.dma("sp", "gdbg", d["dbg_kf"][i * 128:(i + 1) * 128, :], Kf[i][:])
                em.dma("sp", "gdbg", d["dbg_vx"][i * 128:(i + 1) * 128, :], Vx[i][:])
                em.dma("sp", "gdbg", d["dbg_acc"][i * 128:(i + 1) * 128, :], accT[i][:])
                em.dma("sp", "gdbg", d["dbg_P"][i * 128:(i + 1) * 128, :], P_sb[i][:])
            em.dma("sp", "gdbg", d["dbg_mr"], m_run)
            em.dma("sp", "gdbg", d["dbg_it"][:, 0:4096], dbg_it_sb[:, 0:4096])
        if upto == "D":
            _finish()
            return

    Xw = ar.v(112 * KB, 64 * KB)
    xres = [ar.v(176 * KB + i * 4 * KB, 4 * KB, F32) for i in range(2)]
    sqE = ar.v(184 * KB, 2 * KB)
    meanB = ar.v(186 * KB, 2 * KB)
    rstdB_ln = ar.v(188 * KB, 2 * KB)
    rowE = ar.v(0, 4 * KB, F32, (0, 1))
    rowE2 = ar.v(4 * KB, 4 * KB, F32, (0, 1))
    rowE3 = ar.v(190 * KB, 4 * KB, F32, (0, 1))
    a_spots = ([24 * KB + i * KB for i in range(24)]
               + [194 * KB + i * KB for i in range(8)])
    A_ch = [ar.v(a_spots[kc], KB) for kc in range(32)]
    Wq_sb = ar.v(202 * KB, KB)
    keys_sb = ar.v(203 * KB, 128, BF16, (0, 16))

    with ExitStack() as pctx:
        psl = lambda n, s, t: pctx.enter_context(nc.psum_tensor(n, list(s), t))
        psE = [psl(f"psE{i}", (128, 1024), F32) for i in range(2)]
        psS1 = psl("psS1", (1, 1024), F32)
        psS2 = psl("psS2", (1, 1024), F32)

        t_wo = [None] * 34
        t_wo[0], t_wo[1] = t_wo01
        t_xr = [None] * 34
        t_xr[0] = em.dma("sp", "gxr0", xres[0][:], d["xoT"][0:128, :])
        t_xr[1] = em.dma("sp", "gxr1", xres[1][:], d["xoT"][128:256, :])
        A_src = d["A_sta"].transpose([1, 0, 2])
        for kc in range(32):
            em.dma("pool", "gfa", A_ch[kc], A_src[:, kc:kc + 1, :])
        em.dma("pool", "gfa", Wq_sb.rearrange("p (kc m) -> p kc m", m=16),
               d["WqT"].transpose([1, 0, 2]))
        t_fw = em.dma("pool", "gfa", keys_sb, d["keysT"])

        ev_mark = [None] * 32
        mm_mark = [None] * 32
        st_mark = [None] * 32
        for ofc in range(32):
            p = ofc % 2
            em.waits("pe", [t_wo[ofc], t_xr[ofc]])
            if ofc >= 2:
                em.wait("pe", ev_mark[ofc - 2])
            for hc in range(32):
                for hh in range(2):
                    em.opw("pe", [t_wo[ofc]], "matmul",
                           psE[p][:, hh * 512:(hh + 1) * 512],
                           woslab[p][:, hc * 128:(hc + 1) * 128],
                           Q[:, hc * 1024 + hh * 512: hc * 1024 + hh * 512 + 512],
                           start=(hc == 0), stop=(hc == 31))
            mm_mark[ofc] = em.mark("pe")
            em.waits("dve", [mm_mark[ofc], t_xr[ofc]])
            em.op("dve", "tensor_tensor", Xw[:, ofc * 1024:(ofc + 1) * 1024],
                  psE[p][:], xres[p][:], op=ALU.add)
            ev_mark[ofc] = em.mark("dve")
            if ofc + 2 < 32:
                em.wait("pool", mm_mark[ofc])
                t_wo[ofc + 2] = em.dma("pool", f"gwo{p}", woslab[p][:], wo_src[ofc + 2])
                em.wait("sp", ev_mark[ofc])
                t_xr[ofc + 2] = em.dma("sp", f"gxr{p}", xres[p][:],
                                       d["xoT"][(ofc + 2) * 128:(ofc + 3) * 128, :])
            em.wait("act", ev_mark[ofc])
            if ofc >= 1:
                em.wait("act", st_mark[ofc - 1])
            em.op("act", "square", sqE, Xw[:, ofc * 1024:(ofc + 1) * 1024])
            t_sq = em.mark("act")
            em.waits("pe", [ev_mark[ofc], t_sq])
            for hh in range(2):
                em.op("pe", "matmul", psS1[:, hh * 512:(hh + 1) * 512], onec16[:],
                      Xw[:, ofc * 1024 + hh * 512: ofc * 1024 + hh * 512 + 512],
                      start=(ofc == 0), stop=(ofc == 31))
                em.op("pe", "matmul", psS2[:, hh * 512:(hh + 1) * 512], onec16[:],
                      sqE[:, hh * 512:(hh + 1) * 512],
                      start=(ofc == 0), stop=(ofc == 31))
            st_mark[ofc] = em.mark("pe")
        em.wait("dve", st_mark[31])
        em.op("dve", "tensor_scalar", rowE, psS1[:], 1.0 / DM, None, op0=ALU.mult)
        em.op("dve", "tensor_scalar", rowE2, psS2[:], 1.0 / DM, None, op0=ALU.mult)
        em.self_edge("dve")
        em.op("dve", "tensor_tensor", rowE3, rowE, rowE, op=ALU.mult)
        em.self_edge("dve")
        em.op("dve", "tensor_tensor", rowE2, rowE2, rowE3, op=ALU.subtract)
        em.self_edge("dve")
        em.op("dve", "tensor_scalar", rowE2, rowE2, EPS_LN, None, op0=ALU.add)
        t_var = em.mark("dve")
        em.wait("act", t_var)
        em.op("act", "activation", rowE2, rowE2, AF.Ln)
        em.self_edge("act")
        em.op("act", "activation", rowE2, rowE2, AF.Exp, scale=NEGH[0:1, :])
        t_rstd = em.mark("act")
        em.waits("pe", [t_rstd, ev_mark[31], ev_mark[30]])
        for hh in range(2):
            em.op("pe", "matmul", psE[0][:, hh * 512:(hh + 1) * 512], oner32[:, 0:128],
                  rowE[0:1, hh * 512:(hh + 1) * 512], start=True, stop=True)
            em.op("pe", "matmul", psE[1][:, hh * 512:(hh + 1) * 512], oner32[:, 0:128],
                  rowE2[0:1, hh * 512:(hh + 1) * 512], start=True, stop=True)
        t_bc = em.mark("pe")
        em.wait("dve", t_bc)
        em.op("dve", "tensor_copy", meanB, psE[0][:])
        em.op("dve", "tensor_copy", rstdB_ln, psE[1][:])
        em.self_edge("dve")
        for ofc in range(32):
            em.op("dve", "tensor_tensor", Xw[:, ofc * 1024:(ofc + 1) * 1024],
                  Xw[:, ofc * 1024:(ofc + 1) * 1024], meanB, op=ALU.subtract)
        em.self_edge("dve")
        for ofc in range(32):
            em.op("dve", "tensor_tensor", Xw[:, ofc * 1024:(ofc + 1) * 1024],
                  Xw[:, ofc * 1024:(ofc + 1) * 1024], rstdB_ln, op=ALU.mult)
        em.mark("dve")
        em.barrier()
        if debug:
            em.dma("sp", "gdbg", d["dbg_x1"], Xw)
        if upto == "E":
            _finish()
            return

    B_sb = ar.v(48 * KB, 32 * KB)
    zg = [ar.v(80 * KB + i * 2 * KB, 2 * KB) for i in range(4)]
    G_sb = [ar.v(88 * KB + i * 4 * KB, 4 * KB, F32) for i in range(4)]
    qr_sb = ar.v(104 * KB, 2 * KB, BF16, (0, 16))
    gfm = ar.v(106 * KB, 4 * KB, F32, (0, 64))
    R_sb = ar.v(110 * KB, 2 * KB, F32, (0, 64))
    s_tm = ar.v(176 * KB, 2 * KB, F32)
    gate_tm = ar.v(178 * KB, 2 * KB, F32)
    swork = ar.v(180 * KB, 2 * KB, F32)
    gwork = ar.v(182 * KB, 2 * KB, F32)
    ework = ar.v(184 * KB, 2 * KB, F32)
    cur_c = [ar.v(186 * KB + 32 * i, 32, F32) for i in range(2)]
    m1_c = ar.v(186 * KB + 64, 32, F32)
    z_c = ar.v(186 * KB + 96, 32, F32)
    z_c2 = ar.v(186 * KB + 128, 32, F32)
    scf64 = ar.v(187 * KB, 512, F32, (0, 64))
    meanB2 = ar.v(104 * KB, 4 * KB, F32)
    rstdB2 = ar.v(108 * KB, 4 * KB, F32)
    c32 = [ar.v(8 * KB + i * 4 * KB, 4 * KB, F32) for i in range(2)]
    out_sb = [ar.v(16 * KB + i * 4 * KB, 4 * KB, F32) for i in range(2)]

    B_src = d["B_sta"].transpose([1, 0, 2])
    B_dst = B_sb.rearrange("p (kc m) -> p kc m", m=4096)
    t_B = None
    for i in range(4):
        t_B = em.dma("pool", "gb", B_dst[:, i:i + 1, :], B_src[:, i:i + 1, :])
    t_R = em.dma("sp", "gr", R_sb, d["R_exp"])

    with ExitStack() as pctx:
        psl = lambda n, s, t: pctx.enter_context(nc.psum_tensor(n, list(s), t))
        psQR = psl("psQR", (16, 1024), F32)
        psSc = psl("psSc", (64, 1024), F32)
        psT2 = psl("psT2", (128, 64), F32)
        psT3 = psl("psT3", (64, 128), F32)
        psG = psl("psG", (128, 1024), F32)

        em.wait("pe", t_fw)
        for kc in range(32):
            for hh in range(2):
                em.op("pe", "matmul", psQR[0:16, hh * 512:(hh + 1) * 512],
                      Wq_sb[:, kc * 16:(kc + 1) * 16],
                      Xw[:, kc * 1024 + hh * 512: kc * 1024 + hh * 512 + 512],
                      start=(kc == 0), stop=(kc == 31))
        t_qr = em.mark("pe")
        em.wait("dve", t_qr)
        em.op("dve", "tensor_copy", qr_sb, psQR[:])
        t_qre = em.mark("dve")
        em.wait("pe", t_qre)
        for hh in range(2):
            em.opw("pe", [t_fw], "matmul", psSc[0:64, hh * 512:(hh + 1) * 512],
                   keys_sb, qr_sb[:, hh * 512:(hh + 1) * 512],
                   start=True, stop=True)
        t_sc = em.mark("pe")

        tp_mark = [None] * 8
        sev_mark = [None] * 8
        for t in range(8):
            em.wait("dve", t_sc)
            em.op("dve", "tensor_copy", scf64, psSc[0:64, t * 128:(t + 1) * 128])
            t_ev = em.mark("dve")
            em.wait("pe", t_ev)
            if t >= 1:
                em.wait("pe", sev_mark[t - 1])
            em.opw("pe", [t_ev], "transpose", psT2[:], scf64, idn32[0:64, 0:64])
            tp_mark[t] = em.mark("pe")
            em.wait("dve", tp_mark[t])
            em.op("dve", "tensor_copy", s_tm[:, t * 64:(t + 1) * 64], psT2[:])
            sev_mark[t] = em.mark("dve")

        def r3(apx):
            return apx.rearrange("p (t e) -> p t e", e=64)

        def bc4(col):
            return col.rearrange("p t -> p t ()").broadcast_to((128, 4, 64))

        SL = [slice(0, 256), slice(256, 512)]
        CS = [slice(0, 4), slice(4, 8)]
        em.self_edge("dve")
        em.op("dve", "tensor_copy", swork, s_tm)
        for hv in (0, 1):
            em.op("dve", "reduce_max", m1_c[:, CS[hv]], r3(s_tm[:, SL[hv]]),
                  axis=AX)
        for i in range(15):
            pc = m1_c if i == 0 else cur_c[(i - 1) % 2]
            em.self_edge("dve")
            for hv in (0, 1):
                em.op("dve", "tensor_tensor", r3(gwork[:, SL[hv]]),
                      r3(swork[:, SL[hv]]), bc4(pc[:, CS[hv]]), op=ALU.is_ge)
            em.self_edge("dve")
            for hv in (0, 1):
                em.op("dve", "scalar_tensor_tensor", swork[:, SL[hv]],
                      gwork[:, SL[hv]], -1e30, swork[:, SL[hv]],
                      op0=ALU.mult, op1=ALU.add)
            em.self_edge("dve")
            for hv in (0, 1):
                em.op("dve", "reduce_max", cur_c[i % 2][:, CS[hv]],
                      r3(swork[:, SL[hv]]), axis=AX)
        thr = cur_c[14 % 2]
        em.self_edge("dve")
        for hv in (0, 1):
            em.op("dve", "tensor_tensor", r3(ework[:, SL[hv]]),
                  r3(s_tm[:, SL[hv]]), bc4(m1_c[:, CS[hv]]), op=ALU.subtract)
        t_sh = em.mark("dve")
        em.wait("act", t_sh)
        em.op("act", "activation", ework, ework, AF.Exp)
        t_e = em.mark("act")
        em.wait("dve", t_e)
        for hv in (0, 1):
            em.op("dve", "tensor_tensor", r3(gwork[:, SL[hv]]),
                  r3(s_tm[:, SL[hv]]), bc4(thr[:, CS[hv]]), op=ALU.is_ge)
        em.self_edge("dve")
        for hv in (0, 1):
            em.op("dve", "tensor_tensor", ework[:, SL[hv]], ework[:, SL[hv]],
                  gwork[:, SL[hv]], op=ALU.mult)
        em.self_edge("dve")
        for hv in (0, 1):
            em.op("dve", "reduce_sum", z_c[:, CS[hv]], r3(ework[:, SL[hv]]),
                  axis=AX)
        em.self_edge("dve")
        em.op("dve", "reciprocal", z_c2, z_c)
        em.self_edge("dve")
        for hv in (0, 1):
            em.op("dve", "tensor_tensor",
                  r3(gate_tm[:, SL[hv]]), r3(ework[:, SL[hv]]),
                  bc4(z_c2[:, CS[hv]]), op=ALU.mult)
        gt_all = em.mark("dve")

        gfm_ev = [None] * 8
        for t in range(8):
            em.wait("pe", gt_all)
            if t >= 1:
                em.wait("pe", gfm_ev[t - 1])
            em.opw("pe", [gt_all], "transpose", psT3[:],
                   gate_tm[:, t * 64:(t + 1) * 64], idn32[:])
            t_tp = em.mark("pe")
            em.wait("dve", t_tp)
            em.op("dve", "tensor_copy", gfm[:, t * 128:(t + 1) * 128], psT3[:])
            gfm_ev[t] = em.mark("dve")
        em.waits("pe", gfm_ev)
        em.wait("pe", t_R)
        g_ev = [None] * 4
        for c in range(4):
            if c >= 1:
                em.wait("pe", g_ev[c - 1])
            for hh in range(2):
                em.opw("pe", [t_R], "matmul", psG[:, hh * 512:(hh + 1) * 512],
                       R_sb[:, c * 128:(c + 1) * 128],
                       gfm[:, hh * 512:(hh + 1) * 512], start=True, stop=True)
            t_g = em.mark("pe")
            em.wait("dve", t_g)
            em.op("dve", "tensor_copy", G_sb[c][:], psG[:])
            g_ev[c] = em.mark("dve")
        em.barrier()

    with ExitStack() as pctx:
        psl = lambda n, s, t: pctx.enter_context(nc.psum_tensor(n, list(s), t))
        psZ = [psl(f"psZ{i}", (128, 1024), F32) for i in range(2)]
        psS1b = psl("psS1b", (1, 1024), F32)
        psS2b = psl("psS2b", (1, 1024), F32)

        if debug:
            em.dma("sp", "gdbg", d["dbg_gate"], gate_tm)
            em.dma("sp", "gdbg", d["dbg_stm"], s_tm)
        zg_mark = [None] * 4
        for nc4 in range(4):
            p = nc4 % 2
            if nc4 >= 2:
                em.wait("pe", zg_mark[nc4 - 2])
            for kc in range(32):
                for hh in range(2):
                    em.opw("pe", [t_fw], "matmul", psZ[p][:, hh * 512:(hh + 1) * 512],
                           A_ch[kc][:, nc4 * 128:(nc4 + 1) * 128],
                           Xw[:, kc * 1024 + hh * 512: kc * 1024 + hh * 512 + 512],
                           start=(kc == 0), stop=(kc == 31))
            t_z = em.mark("pe")
            em.wait("dve", t_z)
            em.op("dve", "tensor_tensor", zg[nc4], psZ[p][:], G_sb[nc4][:], op=ALU.mult)
            zg_mark[nc4] = em.mark("dve")

        y_ev = [None] * 32
        st2_mark = [None] * 32
        em.waits("pe", zg_mark + [t_B])
        for ofc in range(32):
            p = ofc % 2
            if ofc >= 2:
                em.wait("pe", y_ev[ofc - 2])
            for nc4 in range(4):
                for hh in range(2):
                    em.opw("pe", [t_B], "matmul",
                           psZ[p][:, hh * 512:(hh + 1) * 512],
                           B_sb[:, nc4 * 4096 + ofc * 128: nc4 * 4096 + ofc * 128 + 128],
                           zg[nc4][:, hh * 512:(hh + 1) * 512],
                           start=(nc4 == 0), stop=(nc4 == 3))
            t_y = em.mark("pe")
            em.wait("dve", t_y)
            em.op("dve", "scalar_tensor_tensor", Xw[:, ofc * 1024:(ofc + 1) * 1024],
                  psZ[p][:], FFN_SCALING, Xw[:, ofc * 1024:(ofc + 1) * 1024],
                  op0=ALU.mult, op1=ALU.add)
            y_ev[ofc] = em.mark("dve")
            em.wait("act", y_ev[ofc])
            if ofc >= 1:
                em.wait("act", st2_mark[ofc - 1])
            em.op("act", "square", sqE, Xw[:, ofc * 1024:(ofc + 1) * 1024])
            t_sq = em.mark("act")
            em.waits("pe", [y_ev[ofc], t_sq])
            for hh in range(2):
                em.op("pe", "matmul", psS1b[:, hh * 512:(hh + 1) * 512], onec16[:],
                      Xw[:, ofc * 1024 + hh * 512: ofc * 1024 + hh * 512 + 512],
                      start=(ofc == 0), stop=(ofc == 31))
                em.op("pe", "matmul", psS2b[:, hh * 512:(hh + 1) * 512], onec16[:],
                      sqE[:, hh * 512:(hh + 1) * 512],
                      start=(ofc == 0), stop=(ofc == 31))
            st2_mark[ofc] = em.mark("pe")

        if debug:
            em.waits("sp", [y_ev[31]] + zg_mark)
            for i4 in range(4):
                em.dma("sp", "gdbg", d["dbg_zg"][:, i4 * 1024:(i4 + 1) * 1024], zg[i4])
            em.dma("sp", "gdbg", d["dbg_x2p"], Xw)
        em.wait("dve", st2_mark[31])
        em.op("dve", "tensor_scalar", rowE, psS1b[:], 1.0 / DM, None, op0=ALU.mult)
        em.op("dve", "tensor_scalar", rowE2, psS2b[:], 1.0 / DM, None, op0=ALU.mult)
        em.self_edge("dve")
        em.op("dve", "tensor_tensor", rowE3, rowE, rowE, op=ALU.mult)
        em.self_edge("dve")
        em.op("dve", "tensor_tensor", rowE2, rowE2, rowE3, op=ALU.subtract)
        em.self_edge("dve")
        em.op("dve", "tensor_scalar", rowE2, rowE2, EPS_LN, None, op0=ALU.add)
        t_var = em.mark("dve")
        em.wait("act", t_var)
        em.op("act", "activation", rowE2, rowE2, AF.Ln)
        em.self_edge("act")
        em.op("act", "activation", rowE2, rowE2, AF.Exp, scale=NEGH[0:1, :])
        t_rstd = em.mark("act")
        em.waits("pe", [t_rstd, y_ev[31], y_ev[30]])
        for hh in range(2):
            em.op("pe", "matmul", psZ[0][:, hh * 512:(hh + 1) * 512], oner32[:, 0:128],
                  rowE[0:1, hh * 512:(hh + 1) * 512], start=True, stop=True)
            em.op("pe", "matmul", psZ[1][:, hh * 512:(hh + 1) * 512], oner32[:, 0:128],
                  rowE2[0:1, hh * 512:(hh + 1) * 512], start=True, stop=True)
        t_bc = em.mark("pe")
        em.wait("dve", t_bc)
        em.op("dve", "tensor_copy", meanB2, psZ[0][:])
        em.op("dve", "tensor_copy", rstdB2, psZ[1][:])
        em.self_edge("dve")

        t_out = [None] * 32
        o_mark = [None] * 32
        for ofc in range(32):
            p = ofc % 2
            if ofc >= 2:
                em.waits("dve", [t_out[ofc - 2]])
            em.op("dve", "tensor_tensor", c32[p][:],
                  Xw[:, ofc * 1024:(ofc + 1) * 1024], meanB2, op=ALU.subtract)
            em.self_edge("dve")
            em.op("dve", "tensor_tensor", out_sb[p][:], c32[p][:], rstdB2, op=ALU.mult)
            o_mark[ofc] = em.mark("dve")
            q = "sp" if p == 0 else "pool"
            em.wait(q, o_mark[ofc])
            t_out[ofc] = em.dma(q, f"gout{p}",
                                d["out"][ofc * 128:(ofc + 1) * 128, :], out_sb[p][:])

        for p in range(2):
            em.op("sp", "wait_ge", em.sem_dma[f"gout{p}"], 16 * em.ndma[f"gout{p}"])
        if "gdbg" in em.sem_dma:
            em.op("sp", "wait_ge", em.sem_dma["gdbg"], 16 * em.ndma["gdbg"])


def assemble(res, inp):
    out = np.empty((2, S, DM), dtype=np.float32)
    for core in range(8):
        b, jj = core // 4, core % 4
        tiles = _slot_tiles(jj)
        cols = np.concatenate([np.arange(t * 128, (t + 1) * 128) for t in tiles])
        out[b, cols, :] = np.asarray(res.results[core]["out"]).T
    return out


def kernel(**inputs):
    in_maps = prepare_inputs(inputs)
    nc = build()
    res = run_bass_kernel_spmd(nc, in_maps, core_ids=list(range(8)))
    return assemble(res, inputs)
